# version 23
# baseline (speedup 1.0000x reference)
import numpy as np
import ml_dtypes
import concourse.bass as bass
import concourse.mybir as mybir
from concourse.bass_utils import run_bass_kernel_spmd
from contextlib import ExitStack

F32 = mybir.dt.float32
BF16 = mybir.dt.bfloat16
AF = mybir.ActivationFunctionType
ALU = mybir.AluOpType
AX = mybir.AxisListType

BF = ml_dtypes.bfloat16
EPS_LN = 1e-5
EPS_RMS = float(np.finfo(np.float32).eps)
ENGINES = ("tensor", "vector", "scalar", "gpsimd", "sync")


def make_cfg(S=4096, D=1024, NH=16, NE=8):
    DH = 64
    assert D == NH * DH
    cfg = dict(S=S, D=D, NH=NH, DH=DH, NE=NE)
    cfg["C2"] = NH // 2
    cfg["KI"] = D // 128
    cfg["RC"] = S // 128
    cfg["SO"] = S // 2
    cfg["RCO"] = S // 256
    cfg["QG"] = min(512, cfg["SO"])
    cfg["NQG"] = cfg["SO"] // cfg["QG"]
    cfg["AQ"] = min(512, S)
    cfg["NAP"] = S // cfg["AQ"]
    assert cfg["SO"] % 512 == 0, "own rows must be a multiple of 512"
    cfg["NCORES"] = 8
    return cfg



class Node:
    __slots__ = ("idx", "eng", "fns", "dma", "deps", "cost", "tick",
                 "nwaits")

    def __init__(self, idx, eng, fns, dma, cost):
        self.idx, self.eng, self.fns, self.dma = idx, eng, fns, dma
        self.deps = set()
        self.cost = cost
        self.tick = None


_COST = {"tensor": 400, "vector": 700, "scalar": 750, "gpsimd": 1400,
         "sync": 1500}


class Builder:

    def __init__(self, nc):
        self.nc = nc
        self.cmds = {e: [] for e in ENGINES}
        self.dma_pools = {"ds": [f"ds{i}" for i in range(8)],
                          "dg": [f"dg{i}" for i in range(4)]}
        self.semnames = (["T", "V", "S", "G"]
                         + sum(self.dma_pools.values(), []))
        self.sems = {}
        self.nodes = []
        self.keys = {}

    def seed_key(self, key, pending):
        rec = self.keys.setdefault(key, {"w": None, "r": []})
        rec["r"].extend(pending)

    def key_join(self, keys):
        out = []
        for k in keys:
            rec = self.keys.get(k)
            if not rec:
                continue
            if rec["w"]:
                out.append(rec["w"])
            out.extend(rec["r"])
        return out

    def group(self, eng, fns, reads=(), writes=(), dma=False, cost=None):
        n = Node(len(self.nodes), eng, fns, dma,
                 cost or _COST[eng] * len(fns))
        self.nodes.append(n)
        for k in reads:
            rec = self.keys.setdefault(k, {"w": None, "r": []})
            if rec["w"] is not None:
                n.deps.add(rec["w"].idx)
            rec["r"].append(n)
        for k in writes:
            rec = self.keys.setdefault(k, {"w": None, "r": []})
            if rec["w"] is not None:
                n.deps.add(rec["w"].idx)
            for r in rec["r"]:
                if r is not n:
                    n.deps.add(r.idx)
            self.keys[k] = {"w": n, "r": []}
        return n

    def op(self, eng, fn, reads=(), writes=(), dma=False, cost=None):
        return self.group(eng, [fn], reads=reads, writes=writes, dma=dma,
                          cost=cost)

    def schedule_and_emit(self):
        import heapq
        nodes = self.nodes
        nn = len(nodes)
        children = [[] for _ in range(nn)]
        npend = [0] * nn
        for v in nodes:
            for d in v.deps:
                children[d].append(v.idx)
            npend[v.idx] = len(v.deps)
        finish = [0.0] * nn
        eng_avail = {e: 0.0 for e in ENGINES}
        ready = []
        for v in nodes:
            if npend[v.idx] == 0:
                heapq.heappush(ready, (0.0, v.idx))
        order = []
        est_ready = [0.0] * nn
        while ready:
            _, i = heapq.heappop(ready)
            v = nodes[i]
            start = max(est_ready[i], eng_avail[v.eng])
            fin = start + v.cost
            finish[i] = fin
            eng_avail[v.eng] = fin
            order.append(i)
            for c in children[i]:
                est_ready[c] = max(est_ready[c], fin)
                npend[c] -= 1
                if npend[c] == 0:
                    heapq.heappush(ready, (est_ready[c], c))
        assert len(order) == nn, "cycle in op graph"
        cnt = {sname: 0 for sname in self.semnames}
        rr = {"ds": 0, "dg": 0}
        for i in order:
            v = nodes[i]
            if v.dma:
                pool = "dg" if v.eng == "gpsimd" else "ds"
                names = self.dma_pools[pool]
                sname = names[rr[pool] % len(names)]
                rr[pool] += 1
                prev = cnt[sname]
                cnt[sname] += 16
                v.tick = (sname, cnt[sname], prev)
            else:
                sname = {"tensor": "T", "vector": "V", "scalar": "S",
                         "gpsimd": "G"}[v.eng]
                cnt[sname] += 1
                v.tick = (sname, cnt[sname], None)
        waited = {e: {s: 0 for s in self.semnames} for e in ENGINES}
        for i in order:
            v = nodes[i]
            need = {}
            for d in v.deps:
                s, val, _ = nodes[d].tick
                need[s] = max(need.get(s, 0), val)
            if v.dma and v.tick[2]:
                s = v.tick[0]
                need[s] = max(need.get(s, 0), v.tick[2])
            for s, val in sorted(need.items()):
                if waited[v.eng][s] < val:
                    waited[v.eng][s] = val
                    self.cmds[v.eng].append(
                        lambda E, _s=self.sems[s], _v=val: E.wait_ge(_s, _v))
            sem = self.sems[v.tick[0]]
            step = 16 if v.dma else 1
            nf = len(v.fns)
            for j, fn in enumerate(v.fns):
                def run(E, _fn=fn, _last=(j == nf - 1), _sem=sem,
                        _step=step):
                    inst = _fn(E)
                    if _last:
                        inst.then_inc(_sem, _step)
                self.cmds[v.eng].append(run)
        for eng, pool in (("sync", "ds"), ("gpsimd", "dg")):
            for sname in self.dma_pools[pool]:
                val = cnt[sname]
                if val > 0 and waited[eng][sname] < val:
                    self.cmds[eng].append(
                        lambda E, _s=self.sems[sname], _v=val:
                        E.wait_ge(_s, _v))


class Region:
    def __init__(self, arena, name, lo, hi):
        self.arena, self.name, self.lo, self.hi = arena, name, lo, hi
        self.keys = set()
        self.pending = {}

    def ap(self, lo, hi, plo=0, phi=128):
        assert self.lo + hi <= self.hi, f"{self.name}: {hi} > {self.hi - self.lo}"
        return self.arena.t[plo:phi, self.lo + lo:self.lo + hi]

    def view(self):
        return self.arena.t[:, self.lo:self.hi]

    def k(self, sub=""):
        key = f"{self.name}/{sub}"
        if key not in self.keys:
            self.keys.add(key)
            self.arena.b.seed_key(key, self.pending)
        return key


class Arena:
    def __init__(self, b, tensor, ncols):
        self.b, self.t, self.n = b, tensor, ncols
        self.free_iv = [(0, ncols)]
        self.live = {}
        self.dead = []
        self.peak = 0

    def alloc(self, name, cols):
        for i, (lo, hi) in enumerate(self.free_iv):
            if hi - lo >= cols:
                r = Region(self, name, lo, lo + cols)
                if hi - lo == cols:
                    self.free_iv.pop(i)
                else:
                    self.free_iv[i] = (lo + cols, hi)
                pend = []
                keep = []
                for (dlo, dhi, dp) in self.dead:
                    if dlo < r.hi and dhi > r.lo:
                        pend.extend(dp)
                        if dlo < r.lo:
                            keep.append((dlo, r.lo, dp))
                        if dhi > r.hi:
                            keep.append((r.hi, dhi, dp))
                    else:
                        keep.append((dlo, dhi, dp))
                self.dead = keep
                r.pending = pend
                self.live[name] = r
                used = self.n - sum(h - l for l, h in self.free_iv)
                self.peak = max(self.peak, used)
                return r
        raise RuntimeError(f"arena OOM for {name} ({cols} cols); "
                           f"free={self.free_iv}")

    def free(self, name):
        r = self.live.pop(name)
        pend = self.b.key_join(r.keys)
        self.dead.append((r.lo, r.hi, pend))
        self.free_iv.append((r.lo, r.hi))
        self.free_iv.sort()
        merged = []
        for lo, hi in self.free_iv:
            if merged and merged[-1][1] == lo:
                merged[-1] = (merged[-1][0], hi)
            else:
                merged.append((lo, hi))
        self.free_iv = merged



def build_program(cfg):
    nc = bass.Bass("TRN2", target_bir_lowering=False)
    S, D, KI, C2, RC = cfg["S"], cfg["D"], cfg["KI"], cfg["C2"], cfg["RC"]
    SO, RCO = cfg["SO"], cfg["RCO"]
    QG, NQG, AQ, NAP = cfg["QG"], cfg["NQG"], cfg["AQ"], cfg["NAP"]
    NE = cfg["NE"]
    HW = min(512, D)
    NHO = D // HW
    NQB = QG // 128

    dt = nc.dram_tensor
    x_d = dt("x", [S, D], F32, kind="ExternalInput")
    wq_d = dt("wq", [KI, 128, D], F32, kind="ExternalInput")
    wk_d = dt("wk", [KI, 128, D], F32, kind="ExternalInput")
    wv_d = dt("wv", [KI, 128, D], F32, kind="ExternalInput")
    wo_d = dt("wo", [C2, 128, D], F32, kind="ExternalInput")
    we_d = dt("we", [NE, KI, 128, D], F32, kind="ExternalInput")
    gw_d = dt("gw", [KI, 128, NE], F32, kind="ExternalInput")
    gb_d = dt("gb", [128, NE], F32, kind="ExternalInput")
    eb_d = dt("eb", [NE, D], F32, kind="ExternalInput")
    bqT_d = dt("bqT", [128, C2], F32, kind="ExternalInput")
    bkT_d = dt("bkT", [128, C2], F32, kind="ExternalInput")
    bv_d = dt("bv_bc", [128, D], F32, kind="ExternalInput")
    cosw_d = dt("cosw", [128, S], F32, kind="ExternalInput")
    sinw_d = dt("sinw", [128, S], F32, kind="ExternalInput")
    nw_d = dt("nw128", [128, 1], F32, kind="ExternalInput")
    nwp_d = dt("nwp128", [128, 1], F32, kind="ExternalInput")
    id_d = dt("identbf", [128, 128], BF16, kind="ExternalInput")
    bo_d = dt("bonesbf", [128, 128], BF16, kind="ExternalInput")
    rt_d = dt("rotT", [128, 128], BF16, kind="ExternalInput")
    mk_d = dt("masks_h", [2, NQB, 128, QG], BF16, kind="ExternalInput")
    qsp_d = dt("qsp", [C2, 128, SO], BF16)
    ksp_d = dt("ksp", [C2, 128, S], BF16)
    vsp_d = dt("vsp", [RC, 128, D], BF16)
    zsp_d = dt("zsp", [C2, 128, SO], BF16)
    xnsp_d = dt("xnsp", [128, KI, S], BF16)
    sdram = dt("sdram", [SO, D], F32)
    webf_d = dt("webf", [NE, KI, 128, D], BF16)
    out_d = dt("out", [SO, D], F32, kind="ExternalOutput")

    es = ExitStack()
    sbt = lambda name, shape, dtp: es.enter_context(
        nc.sbuf_tensor("s_" + name, shape, dtp))
    pst = lambda name, shape, dtp: es.enter_context(
        nc.psum_tensor(name, shape, dtp))

    with es:
        ident = sbt("ident", [128, 128], BF16)
        bones = sbt("bones", [128, 128], BF16)
        rotT = sbt("rott", [128, 128], BF16)
        masks = sbt("masks", [128, 2 * NQB * QG], BF16)
        stats_rs = sbt("stats_rs", [128, RC], F32)
        stats_nb = sbt("stats_nb", [128, RC], F32)
        bqT = sbt("bqT", [128, C2], F32)
        bkT = sbt("bkT", [128, C2], F32)
        bv_bc = sbt("bv_bc", [128, D], F32)
        gb_bc = sbt("gb_bc", [128, NE], F32)
        nw128 = sbt("nw128", [128, 1], F32)
        nwp128 = sbt("nwp128", [128, 1], F32)
        gw_sb = sbt("gw_sb", [128, KI * NE], BF16)
        eb_sb = sbt("eb_sb", [128, D], BF16)
        g_all = sbt("g_all", [128, RCO * NE], F32)
        gbf = sbt("gbf", [128, NE], BF16)
        gt_sb = sbt("gt_sb", [128, SO], BF16)
        tiny = sbt("tiny", [128, 16], F32)
        cst = sbt("cst", [128, 4], F32)
        gltmp = sbt("gltmp", [128, 4 * NE], F32)

        need_a_bf = (3 * KI * D + 2 * KI * AQ + 2 * D + 12 * 512
                     + 2 * C2 * 512 + 2 * D)
        need_b_bf = (2 * (2 * S + RC * 256) + 2 * C2 * 512 + 8 * QG
                     + C2 * D + 2 * QG + 2 * C2 * 128)
        NDH = min(4, RCO)
        NRH = RCO // NDH
        need_d_bf = (2 * KI * D + KI * SO)
        NBF = max(need_a_bf, need_b_bf, need_d_bf) + 6144
        need_a_f = 4 * AQ + 4 * D + 16 * 512
        need_de_f = 2 * D + 2 * NRH * D + 6 * D
        NF32 = max(need_a_f, need_de_f) + 256
        abf_t = sbt("abf", [128, NBF], BF16)
        af_t = sbt("af32", [128, NF32], F32)

        class PView:
            def __init__(self, t, off):
                self.t, self.off = t, off
            def __getitem__(self, idx):
                ps, cs = idx
                a = cs.start or 0
                bnd = cs.stop if cs.stop is not None else 512
                return self.t[ps, self.off + a:self.off + bnd]

        p01 = pst("p01", [128, 1024], F32)
        p23 = pst("p23", [128, 1024], F32)
        p45 = pst("p45", [128, 1024], F32)
        p6 = pst("p6", [128, 512], F32)
        pbt = pst("pbt", [128, 1024], BF16)
        pf = [PView(p01, 0), PView(p01, 512), PView(p23, 0), PView(p23, 512),
              PView(p45, 0), PView(p45, 512), PView(p6, 0)]
        pfull = [p01, p23, p45]

        b = Builder(nc)
        for sn in b.semnames:
            b.sems[sn] = es.enter_context(nc.semaphore(sn))
        ABF = Arena(b, abf_t, NBF)
        AF32 = Arena(b, af_t, NF32)

        def ld(eng, dst, src, key):
            b.op(eng, lambda E, _d=dst, _s=src: E.dma_start(out=_d, in_=_s),
                 writes=[key], dma=True)

        ld("sync", ident[:, :], id_d[:, :], "ident")
        ld("sync", bones[:, :], bo_d[:, :], "bones")
        ld("sync", rotT[:, :], rt_d[:, :], "rotT")
        ld("sync", masks[:, :].rearrange("p (c m q) -> p c m q", c=2, m=NQB),
           mk_d.rearrange("c m p q -> p c m q"), "masks")
        ld("sync", bqT[:, :], bqT_d[:, :], "bqT")
        ld("sync", bkT[:, :], bkT_d[:, :], "bkT")
        ld("sync", bv_bc[:, :], bv_d[:, :], "bv_bc")
        ld("sync", gb_bc[:, :], gb_d[:, :], "gb_bc")
        ld("sync", nw128[:, :], nw_d[:, :], "nw128")
        ld("sync", nwp128[:, :], nwp_d[:, :], "nwp128")
        ld("gpsimd", gw_sb[:, :].rearrange("p (k e) -> p k e", k=KI),
           gw_d.rearrange("k p e -> p k e"), "gw_sb")
        ld("gpsimd", eb_sb[0:NE, :], eb_d[:, :], "eb_sb")
        b.op("vector", lambda E: E.memset(cst[:, 0:1], EPS_LN), writes=["cst0"])
        b.op("vector", lambda E: E.memset(cst[:, 1:2], EPS_RMS), writes=["cst1"])
        b.op("vector", lambda E: E.memset(cst[:, 2:3], 1.0 / 64), writes=["cst2"])
        b.op("vector", lambda E: E.memset(cst[:, 3:4], -0.5), writes=["cst3"])

        wq_r = ABF.alloc("wq", KI * D)
        wk_r = ABF.alloc("wk", KI * D)
        wv_r = ABF.alloc("wv", KI * D)
        xnT_q2 = [ABF.alloc(f"xnT_q{i}", KI * AQ) for i in range(2)]
        xnbf = [ABF.alloc(f"xnbf{i}", D) for i in range(2)]
        sqb = [ABF.alloc(f"sqb{i}", 512) for i in range(4)]
        qstage = ABF.alloc("qstage", C2 * 512)
        kstage = ABF.alloc("kstage", C2 * 512)
        vstage = [ABF.alloc(f"vstage{i}", D) for i in range(2)]
        cosw2 = [AF32.alloc(f"cosw{i}", AQ) for i in range(2)]
        sinw2 = [AF32.alloc(f"sinw{i}", AQ) for i in range(2)]
        xbuf = [AF32.alloc(f"xbuf{i}", D) for i in range(2)]
        xsq2 = [AF32.alloc(f"xsq{i}", D) for i in range(2)]
        srtb = [AF32.alloc(f"srtb{i}", 512) for i in range(4)]
        rstdb = [AF32.alloc(f"rstdb{i}", 512) for i in range(4)]
        qnb = [ABF.alloc(f"qnb{i}", 512) for i in range(4)]
        t1b = [AF32.alloc(f"t1b{i}", 512) for i in range(4)]
        t2b = [AF32.alloc(f"t2b{i}", 512) for i in range(4)]

        for wrg, wd in ((wq_r, wq_d), (wk_r, wk_d), (wv_r, wv_d)):
            ld("gpsimd", wrg.view().rearrange("p (k o) -> p k o", k=KI),
               wd.rearrange("k p o -> p k o"), wrg.k())

        def qk_tile(wreg, bcol, bkey, c, rb, grb, sl, isq, xnT_q, cosw, sinw,
                    psl):
            pp, kpp = pf[psl % 4], f"pf{psl % 4}"
            vb, kvb = pf[4 + psl % 2], f"pf{4 + psl % 2}"
            xnt_keys = [xnT_q.k(f"r{rr}") for rr in
                        range(rb // 128, rb // 128 + 4)]
            ops = []
            fns = []
            for ki in range(KI):
                fns.append(lambda E, _ki=ki, _c=c, _rb=rb, _w=wreg, _pp=pp,
                           _x=xnT_q: E.matmul(
                    _pp[:, :],
                    _w.ap(_ki * D + _c * 128, _ki * D + (_c + 1) * 128),
                    _x.ap(_ki * AQ + _rb, _ki * AQ + _rb + 512),
                    start=(_ki == 0), stop=(_ki == KI - 1)))
            ops.append(("tensor", fns, [wreg.k()] + xnt_keys, [kpp]))
            ops.append(("scalar", [lambda E, _sl=sl, _pp=pp, _b=bcol, _c=c:
                 E.activation(sqb[_sl].view(), _pp[:, :], AF.Square,
                              bias=_b[:, _c:_c + 1])],
                 [kpp, bkey], [sqb[sl].k()]))
            ops.append([("tensor", [lambda E, _sl=sl, _vb=vb: E.matmul(
                _vb[:, :], bones[:, :], sqb[_sl].view(),
                start=True, stop=True)],
                [sqb[sl].k(), "bones"], [kvb]),
                ("scalar", [lambda E, _sl=sl, _vb=vb: E.activation(
                    srtb[_sl].view(), _vb[:, :], AF.Ln,
                    bias=cst[:, 1:2], scale=cst[:, 2:3])],
                    [kvb, "cst1", "cst2"], [srtb[sl].k()])])
            ops.append(("scalar", [lambda E, _sl=sl: E.activation(
                rstdb[_sl].view(), srtb[_sl].view(), AF.Exp,
                scale=cst[:, 3:4])],
                [srtb[sl].k(), "cst3"], [rstdb[sl].k()]))
            ops.append(("vector", [lambda E, _sl=sl, _pp=pp, _b=bcol, _c=c:
                 E.scalar_tensor_tensor(
                     qnb[_sl].view(), _pp[:, :], _b[:, _c:_c + 1],
                     rstdb[_sl].view(), ALU.add, ALU.mult)],
                 [kpp, rstdb[sl].k(), bkey], [qnb[sl].k()]))
            ops.append(("tensor", [lambda E, _sl=sl, _pp=pp: E.matmul(
                _pp[:, :], rotT[:, :], qnb[_sl].view(),
                start=True, stop=True)],
                [qnb[sl].k(), "rotT"], [kpp]))
            ops.append(("gpsimd", [lambda E, _sl=sl, _rb=rb, _cw=cosw:
                 E.tensor_tensor(
                     t1b[_sl].view(), qnb[_sl].view(), _cw.ap(_rb, _rb + 512),
                     ALU.mult)],
                 [qnb[sl].k(), cosw.k()], [t1b[sl].k()]))
            ops.append(("vector", [lambda E, _sl=sl, _rb=rb, _sw=sinw,
                 _pp=pp: E.tensor_tensor(
                     t2b[_sl].view(), _pp[:, :], _sw.ap(_rb, _rb + 512),
                     ALU.mult)],
                 [kpp, sinw.k()], [t2b[sl].k()]))
            stg, skey = (qstage, qstage.k(f"c{c}")) if isq else                 (kstage, kstage.k(f"c{c}"))
            ops.append(("gpsimd", [lambda E, _sl=sl, _c=c, _st=stg:
                 E.tensor_tensor(
                     _st.ap(_c * 512, (_c + 1) * 512),
                     t1b[_sl].view(), t2b[_sl].view(), ALU.add)],
                 [t1b[sl].k(), t2b[sl].k()], [skey]))
            return ops

        def emit_interleaved(tile_ops, width=4):
            for base in range(0, len(tile_ops), width):
                grp = tile_ops[base:base + width]
                nst = max(len(t) for t in grp)
                for st in range(nst):
                    for t in grp:
                        if st < len(t):
                            specs = t[st]
                            if isinstance(specs, tuple):
                                specs = [specs]
                            for spec in specs:
                                if len(spec) == 5:
                                    eng, fns, rd, wr, dma = spec
                                else:
                                    eng, fns, rd, wr = spec
                                    dma = False
                                b.group(eng, fns, reads=rd, writes=wr,
                                        dma=dma)

        def fuse_proj(tiles):
            for i in range(0, len(tiles) - 1, 2):
                ea, fa, ra, wa = tiles[i][0]
                eb, fb, rbb, wb = tiles[i + 1][0]
                fz = [f for pair in zip(fa, fb) for f in pair]
                tiles[i][0] = (ea, fz, list(dict.fromkeys(ra + rbb)),
                               wa + wb)
                tiles[i + 1][0] = []

        tctr = [0]
        for ap_i in range(NAP):
            rowbase = ap_i * AQ
            asl = ap_i % 2
            xnT_q = xnT_q2[asl]
            cosw, sinw = cosw2[asl], sinw2[asl]
            b.op("sync", lambda E, _r=rowbase, _c=cosw: E.dma_start(
                out=_c.view(), in_=cosw_d[:, _r:_r + AQ]),
                writes=[cosw.k()], dma=True)
            b.op("sync", lambda E, _r=rowbase, _s=sinw: E.dma_start(
                out=_s.view(), in_=sinw_d[:, _r:_r + AQ]),
                writes=[sinw.k()], dma=True)
            b.op("vector", lambda E, _c=cosw: E.tensor_scalar_mul(
                _c.view(), _c.view(), nw128[:, 0:1]),
                reads=[cosw.k(), "nw128"], writes=[cosw.k()])
            b.op("vector", lambda E, _s=sinw: E.tensor_scalar_mul(
                _s.view(), _s.view(), nwp128[:, 0:1]),
                reads=[sinw.k(), "nwp128"], writes=[sinw.k()])

            def ln_tile(r, rc, xnT_q):
                sl = r % 2
                tb = 8 * sl
                kx = xbuf[sl].k()
                xsq = xsq2[sl]
                T = lambda j: tiny[:, tb + j:tb + j + 1]
                ops = []
                ops.append(("sync", [lambda E, _rc=rc, _sl=sl: E.dma_start(
                    out=xbuf[_sl].view(),
                    in_=x_d[_rc * 128:(_rc + 1) * 128, :])],
                    [], [kx], True))
                ops.append([
                    ("vector", [lambda E, _sl=sl: E.reduce_sum(
                        T(0), xbuf[_sl].view(), AX.X)], [kx], [f"t0_{sl}"]),
                    ("scalar", [lambda E, _sl=sl, _x=xsq: E.activation(
                        _x.view(), xbuf[_sl].view(), AF.Square)],
                        [kx], [xsq.k()])])
                ops.append(("vector", [lambda E, _x=xsq: E.reduce_sum(
                    T(1), _x.view(), AX.X)], [xsq.k()], [f"t1_{sl}"]))
                ops.append([
                    ("vector", [lambda E: E.tensor_scalar_mul(
                        T(2), T(0), 1.0 / D)], [f"t0_{sl}"], [f"t2_{sl}"]),
                    ("vector", [lambda E: E.tensor_tensor(
                        T(3), T(2), T(2), ALU.mult)], [f"t2_{sl}"],
                        [f"t3_{sl}"]),
                    ("vector", [lambda E: E.scalar_tensor_tensor(
                        T(4), T(1), 1.0 / D, T(3), ALU.mult, ALU.subtract)],
                        [f"t1_{sl}", f"t3_{sl}"], [f"t4_{sl}"])])
                ops.append([
                    ("scalar", [lambda E: E.activation(
                        T(5), T(4), AF.Ln, bias=cst[:, 0:1])],
                        [f"t4_{sl}", "cst0"], [f"t5_{sl}"]),
                    ("scalar", [lambda E, _rc=rc: E.activation(
                        stats_rs[:, _rc:_rc + 1], T(5), AF.Exp,
                        scale=cst[:, 3:4])],
                        [f"t5_{sl}", "cst3"], [f"rs{rc}"])])
                ops.append(("vector", [lambda E, _rc=rc: E.scalar_tensor_tensor(
                    stats_nb[:, _rc:_rc + 1], T(2), -1.0,
                    stats_rs[:, _rc:_rc + 1], ALU.mult, ALU.mult)],
                    [f"t2_{sl}", f"rs{rc}"], [f"nb{rc}"]))
                ops.append(("vector", [lambda E, _sl=sl, _rc=rc: E.tensor_scalar(
                    xnbf[_sl].view(), xbuf[_sl].view(),
                    stats_rs[:, _rc:_rc + 1], stats_nb[:, _rc:_rc + 1],
                    ALU.mult, ALU.add)],
                    [kx, f"rs{rc}", f"nb{rc}"], [xnbf[sl].k()]))
                fns = []
                for ki in range(KI):
                    fns.append(lambda E, _sl=sl, _ki=ki: E.transpose(
                        pbt[:, _ki * 128:(_ki + 1) * 128],
                        xnbf[_sl].ap(_ki * 128, (_ki + 1) * 128), ident[:, :]))
                ops.append([
                    ("tensor", fns, [xnbf[sl].k(), "ident"], ["pbt"]),
                    ("vector", [lambda E, _r=r, _x=xnT_q: E.tensor_copy(
                        _x.view().rearrange("p (k n) -> p k n", k=KI)
                        [:, :, _r * 128:(_r + 1) * 128],
                        pbt[:, 0:KI * 128].rearrange("p (k n) -> p k n",
                                                     k=KI))],
                        ["pbt"], [xnT_q.k(f"r{r}")])])
                return ops

            nrc = AQ // 128
            ln_tiles = [ln_tile(r, ap_i * nrc + r, xnT_q) for r in range(nrc)]
            emit_interleaved(ln_tiles, width=2)

            for g in range(AQ // 512):
                rb = g * 512
                grb = rowbase + rb
                tiles = []
                for c in range(C2):
                    tiles.append(qk_tile(wk_r, bkT, "bkT", c, rb, grb,
                                         tctr[0] % 4, False, xnT_q, cosw,
                                         sinw, tctr[0]))
                    tctr[0] += 1
                fuse_proj(tiles)
                emit_interleaved(tiles)
                b.op("sync", lambda E, _g=grb: E.dma_start(
                    out=ksp_d[:, :, _g:_g + 512].rearrange("c p r -> p c r"),
                    in_=kstage.view().rearrange("p (c r) -> p c r", c=C2)),
                    reads=[kstage.k(f"c{c}") for c in range(C2)],
                    writes=[f"ksp{grb}"], dma=True)
                if grb < SO:
                    tiles = []
                    for c in range(C2):
                        tiles.append(qk_tile(wq_r, bqT, "bqT", c, rb, grb,
                                             tctr[0] % 4, True, xnT_q, cosw,
                                             sinw, tctr[0]))
                        tctr[0] += 1
                    fuse_proj(tiles)
                    emit_interleaved(tiles)
                    b.op("sync", lambda E, _g=grb: E.dma_start(
                        out=qsp_d[:, :, _g:_g + 512].rearrange(
                            "c p r -> p c r"),
                        in_=qstage.view().rearrange("p (c r) -> p c r", c=C2)),
                        reads=[qstage.k(f"c{c}") for c in range(C2)],
                        writes=[f"qsp{grb}"], dma=True)
                for r4 in range(4):
                    rc = (grb + r4 * 128) // 128
                    vs = vstage[rc % 2]
                    for half in range(NHO):
                        pi = 6
                        fns = []
                        for ki in range(KI):
                            fns.append(lambda E, _ki=ki, _rb=rb, _r4=r4,
                                       _h=half, _pi=pi, _x=xnT_q: E.matmul(
                                pf[_pi][:, 0:HW],
                                _x.ap(_ki * AQ + _rb + _r4 * 128,
                                      _ki * AQ + _rb + _r4 * 128 + 128),
                                wv_r.ap(_ki * D + _h * HW,
                                        _ki * D + (_h + 1) * HW),
                                start=(_ki == 0), stop=(_ki == KI - 1)))
                        b.group("tensor", fns,
                                reads=[wv_r.k(), xnT_q.k(f"r{rb // 128 + r4}")],
                                writes=[f"pf{pi}"])
                        b.op("vector", lambda E, _h=half, _pi=pi, _vs=vs:
                             E.tensor_tensor(
                                 _vs.ap(_h * HW, (_h + 1) * HW),
                                 pf[_pi][:, 0:HW],
                                 bv_bc[:, _h * HW:(_h + 1) * HW], ALU.add),
                             reads=[f"pf{pi}", "bv_bc"],
                             writes=[vs.k(str(half))])
                    b.op("sync", lambda E, _rc=rc, _vs=vs: E.dma_start(
                        out=vsp_d[_rc, :, :], in_=_vs.view()),
                        reads=[vs.k(str(h)) for h in range(NHO)],
                        writes=[f"vsp{rc}"], dma=True)
            for ki in range(KI):
                b.op("sync", lambda E, _ki=ki, _rb=rowbase, _x=xnT_q:
                     E.dma_start(
                         out=xnsp_d[:, _ki, _rb:_rb + AQ],
                         in_=_x.ap(_ki * AQ, (_ki + 1) * AQ)),
                     reads=[xnT_q.k(f"r{rr}") for rr in range(AQ // 128)],
                     writes=[f"xnsp{ki}_{ap_i}"], dma=True)

        for nm in (["wq", "wk", "wv", "xnT_q0", "xnT_q1", "xnbf0", "xnbf1",
                    "qstage", "kstage", "vstage0", "vstage1"]
                   + [f"sqb{i}" for i in range(4)]
                   + [f"qnb{i}" for i in range(4)]):
            ABF.free(nm)
        for nm in (["cosw0", "cosw1", "sinw0", "sinw1", "xbuf0", "xbuf1",
                    "xsq0", "xsq1"]
                   + [f"{p}{i}" for p in ("srtb", "rstdb", "t1b",
                                          "t2b") for i in range(4)]):
            AF32.free(nm)

        qT_sl = [ABF.alloc(f"qTsl{i}", QG) for i in range(2)]
        kT_w = [ABF.alloc(f"kTw{i}", S) for i in range(2)]
        v_w = [ABF.alloc(f"vw{i}", RC * 128) for i in range(2)]
        patt = [ABF.alloc(f"patt{i}", 2 * QG) for i in range(4)]
        zst = [ABF.alloc(f"zst{i}", QG) for i in range(2)]
        woT = ABF.alloc("woT", C2 * D)
        sstage = [AF32.alloc(f"sstage{i}", D) for i in range(2)]

        ld("gpsimd", woT.view().rearrange("p (c o) -> p c o", c=C2),
           wo_d.rearrange("c p o -> p c o"), woT.k())

        evac_rr = [0]
        for w in range(C2):
            wsl = w % 2
            kkw = kT_w[wsl].k()
            kvw = v_w[wsl].k()
            kkw_keys = {g: kkw for g in range(0, S, 512)}
            kvw_keys = {g: kvw for g in range(0, S, 512)}
            b.op("sync", lambda E, _w=w, _s=wsl: E.dma_start(
                out=kT_w[_s].view(), in_=ksp_d[_w, :, :]),
                reads=[f"ksp{g}" for g in range(0, S, 512)],
                writes=[kkw], dma=True)
            b.op("sync", lambda E, _w=w, _s=wsl: E.dma_start(
                out=v_w[_s].view().rearrange("p (r c) -> p r c", r=RC),
                in_=vsp_d[:, :, _w * 128:(_w + 1) * 128].rearrange(
                    "r p c -> p r c")),
                reads=[f"vsp{rc}" for rc in range(RC)],
                writes=[kvw], dma=True)
            for qg in range(NQG):
                qb = qg * QG
                sl = qg % 2
                kqT = qT_sl[sl].k()
                b.op("sync", lambda E, _sl=sl, _qb=qb, _w=w: E.dma_start(
                    out=qT_sl[_sl].ap(0, QG),
                    in_=qsp_d[_w, :, _qb:_qb + QG]),
                    reads=[f"qsp{g}" for g in range(qb, qb + QG, 512)],
                    writes=[kqT], dma=True)
                imax = (qb + QG) // 128 - 1
                klist = ([("own", j) for j in range(imax + 1)] +
                         [("par", j) for j in range(imax + 1)])
                pend = []
                for kn, (cls, j) in enumerate(klist):
                    kb = j if cls == "own" else RCO + j
                    inb = j - qb // 128
                    masked = 0 <= inb < NQB
                    moff = ((0 if cls == "own" else NQB) + inb) * QG
                    sp = pfull[1 + kn % 2]
                    kb0, kb1 = (("pf2", "pf3") if kn % 2 == 0
                                else ("pf4", "pf5"))
                    ring = kn % 4
                    pat = patt[ring]
                    kp = pat.k()
                    b.group("tensor", [
                        (lambda E, _u=u, _kb=kb, _sl=sl,
                         _sp=sp, _ws=wsl: E.matmul(
                             _sp[:, _u * QG:(_u + 1) * QG],
                             kT_w[_ws].ap(_kb * 128, (_kb + 1) * 128,
                                          64 * _u, 64 * _u + 64),
                             qT_sl[_sl].ap(0, QG, 64 * _u, 64 * _u + 64),
                             start=True, stop=True,
                             tile_position=(64 * _u, 0)))
                        for u in range(2)],
                        reads=[kkw_keys[(kb // 4) * 512], kqT],
                        writes=[kb0, kb1])
                    e = evac_rr[0] % 8
                    evac_rr[0] += 1
                    if e in (0, 1, 2, 4, 5, 7):
                        b.op("scalar", lambda E, _sp=sp, _pat=pat:
                             E.activation(_pat.ap(0, 2 * QG),
                                          _sp[:, 0:2 * QG], AF.Square),
                             reads=[kb0, kb1], writes=[kp])
                    elif e == 6:
                        b.op("vector", lambda E, _sp=sp, _pat=pat:
                             E.tensor_copy(_pat.ap(0, 2 * QG),
                                           _sp[:, 0:2 * QG]),
                             reads=[kb0, kb1], writes=[kp])
                        b.op("gpsimd", lambda E, _pat=pat:
                             E.tensor_tensor(_pat.ap(0, 2 * QG),
                                             _pat.ap(0, 2 * QG),
                                             _pat.ap(0, 2 * QG), ALU.mult),
                             reads=[kp], writes=[kp])
                    else:
                        b.op("vector", lambda E, _sp=sp, _pat=pat:
                             E.tensor_copy(_pat.ap(0, 2 * QG),
                                           _sp[:, 0:2 * QG]),
                             reads=[kb0, kb1], writes=[kp])
                        b.op("vector", lambda E, _pat=pat:
                             E.tensor_tensor(_pat.ap(0, 2 * QG),
                                             _pat.ap(0, 2 * QG),
                                             _pat.ap(0, 2 * QG), ALU.mult),
                             reads=[kp], writes=[kp])
                    if masked:
                        b.op("vector", lambda E, _pat=pat, _m=moff:
                             E.tensor_tensor(
                                 _pat.ap(0, 2 * QG).rearrange(
                                     "p (u q) -> p u q", u=2),
                                 _pat.ap(0, 2 * QG).rearrange(
                                     "p (u q) -> p u q", u=2),
                                 masks[:, _m:_m + QG].rearrange(
                                     "p q -> p () q").broadcast_to(
                                         [128, 2, QG]),
                                 ALU.mult),
                             reads=[kp, "masks"], writes=[kp])
                    def emit_z(_kb=kb, _pat=pat, _kp=kp,
                               _first=(kn == 0),
                               _last=(kn == len(klist) - 1), _ws=wsl,
                               _kvw=kvw_keys[(kb // 4) * 512]):
                        b.group("tensor", [
                            (lambda E, _u=u: E.matmul(
                                p01[64 * _u:64 * _u + 64,
                                    _u * QG:(_u + 1) * QG],
                                v_w[_ws].ap(_kb * 128 + _u * 64,
                                            _kb * 128 + _u * 64 + 64),
                                _pat.ap(_u * QG, (_u + 1) * QG),
                                start=_first, stop=_last,
                                tile_position=(0, 64 * _u),
                                skip_group_check=True))
                            for u in range(2)],
                            reads=[_kp, _kvw, "pf0", "pf1"],
                            writes=["pf0", "pf1"])
                    pend.append(emit_z)
                    if len(pend) > 3:
                        pend.pop(0)()
                for f in pend:
                    f()
                zs = zst[qg % 2]
                b.op("vector", lambda E, _zs=zs: E.tensor_copy(
                    _zs.ap(0, QG, 0, 64), p01[0:64, 0:QG]),
                    reads=["pf0"], writes=[zs.k("a")])
                b.op("vector", lambda E, _zs=zs: E.tensor_copy(
                    _zs.ap(0, QG, 64, 128), p01[64:128, QG:2 * QG]),
                    reads=["pf1"], writes=[zs.k("b")])
                b.op("sync", lambda E, _hp=w, _qb=qb, _zs=zs: E.dma_start(
                    out=zsp_d[_hp, :, _qb:_qb + QG], in_=_zs.view()),
                    reads=[zs.k("a"), zs.k("b")],
                    writes=[f"zsp{w}_{qb}"], dma=True)

        for e in range(NE):
            b.op("gpsimd", lambda E, _e=e: E.dma_start(
                out=webf_d[_e], in_=we_d[_e]),
                writes=[f"webf{e}"], dma=True)

        zrc = [ABF.alloc(f"zrc{i}", C2 * 128) for i in range(2)]
        for rc in range(RCO):
            sl = rc % 2
            zr = zrc[sl]
            b.op("sync", lambda E, _rc=rc, _zr=zr: E.dma_start(
                out=_zr.view().rearrange("p (c r) -> p c r", c=C2),
                in_=zsp_d[:, :, _rc * 128:(_rc + 1) * 128].rearrange(
                    "c p r -> p c r")),
                reads=[f"zsp{hp}_{(rc * 128 // QG) * QG}" for hp in range(C2)],
                writes=[zr.k()], dma=True)
            fns = []
            for hp in range(C2):
                for half in range(NHO):
                    fns.append(lambda E, _hp=hp, _rc=rc, _h=half,
                               _pp=pf[2 + half % 2], _zr=zr: E.matmul(
                        _pp[:, 0:HW],
                        _zr.ap(_hp * 128, (_hp + 1) * 128),
                        woT.ap(_hp * D + _h * HW, _hp * D + (_h + 1) * HW),
                        start=(_hp == 0), stop=(_hp == C2 - 1)))
            b.group("tensor", fns, reads=[zr.k(), woT.k()],
                    writes=["pf2", "pf3"])
            for half in range(NHO):
                pp, kk = pf[2 + half % 2], f"pf{2 + half % 2}"
                if half % 2 == 0:
                    b.op("vector", lambda E, _sl=sl, _pp=pp, _h=half:
                         E.tensor_copy(
                             sstage[_sl].ap(_h * HW, (_h + 1) * HW),
                             _pp[:, 0:HW]),
                         reads=[kk], writes=[sstage[sl].k(str(half))])
                else:
                    b.op("scalar", lambda E, _sl=sl, _pp=pp, _h=half: E.copy(
                        sstage[_sl].ap(_h * HW, (_h + 1) * HW), _pp[:, 0:HW]),
                        reads=[kk], writes=[sstage[sl].k(str(half))])
            b.op("sync", lambda E, _sl=sl, _rc=rc: E.dma_start(
                out=sdram[_rc * 128:(_rc + 1) * 128, :],
                in_=sstage[_sl].view()),
                reads=[sstage[sl].k(str(h)) for h in range(NHO)],
                writes=[f"sd{rc}"], dma=True)

        for nm in ("qTsl0", "qTsl1", "kTw0", "kTw1", "vw0", "vw1", "woT",
                   "zst0", "zst1", "zrc0", "zrc1"):
            ABF.free(nm)
        for nm in [f"patt{i}" for i in range(4)]:
            ABF.free(nm)
        AF32.free("sstage0")
        AF32.free("sstage1")

        we_r = [ABF.alloc(f"we{i}", KI * D) for i in range(2)]
        xnT_dh = ABF.alloc("xnT_dh", KI * SO)
        xnd_keys = []
        for ki in range(KI):
            kxk = xnT_dh.k(f"{ki}")
            xnd_keys.append(kxk)
            b.op("sync", lambda E, _ki=ki: E.dma_start(
                out=xnT_dh.ap(_ki * SO, (_ki + 1) * SO),
                in_=xnsp_d[:, _ki, 0:SO]),
                reads=[f"xnsp{ki}_{a}" for a in range(NAP)],
                writes=[kxk], dma=True)
        accr2 = [AF32.alloc(f"accr{i}", NRH * D) for i in range(2)]
        srow = [AF32.alloc(f"srow{i}", D) for i in range(2)]
        xrow = [AF32.alloc(f"xrow{i}", D) for i in range(2)]
        xnrow = AF32.alloc("xnrow", D)
        gsil = AF32.alloc("gsil", D)

        GL = gltmp
        for rc in range(RCO):
            fns = []
            for ki in range(KI):
                fns.append(lambda E, _ki=ki, _rc=rc: E.matmul(
                    pf[6][:, 0:NE],
                    xnT_dh.ap(_ki * SO + _rc * 128, _ki * SO + (_rc + 1) * 128),
                    gw_sb[:, _ki * NE:(_ki + 1) * NE],
                    start=(_ki == 0), stop=(_ki == KI - 1)))
            b.group("tensor", fns, reads=xnd_keys + ["gw_sb"], writes=["pf6"])
            b.op("vector", lambda E: E.tensor_tensor(
                GL[:, 0:NE], pf[6][:, 0:NE], gb_bc[:, :], ALU.add),
                reads=["pf6", "gb_bc"], writes=["glA"])
            b.op("vector", lambda E: E.reduce_max(
                GL[:, NE:NE + 1], GL[:, 0:NE], AX.X),
                reads=["glA"], writes=["glB"])
            b.op("vector", lambda E: E.tensor_single_scalar(
                GL[:, NE + 2:2 * NE + 2], GL[:, 0:NE], GL[:, NE:NE + 1],
                ALU.is_equal),
                reads=["glA", "glB"], writes=["glD"])
            b.op("vector", lambda E: E.scalar_tensor_tensor(
                GL[:, 2 * NE + 2:3 * NE + 2], GL[:, NE + 2:2 * NE + 2],
                -1e30, GL[:, 0:NE], ALU.mult, ALU.add),
                reads=["glD", "glA"], writes=["glE"])
            b.op("vector", lambda E: E.reduce_max(
                GL[:, NE + 1:NE + 2], GL[:, 2 * NE + 2:3 * NE + 2], AX.X),
                reads=["glE"], writes=["glC"])
            b.op("vector", lambda E: E.tensor_single_scalar(
                GL[:, 2 * NE + 2:3 * NE + 2], GL[:, 0:NE],
                GL[:, NE + 1:NE + 2], ALU.is_ge),
                reads=["glA", "glC"], writes=["glE"])
            b.op("vector", lambda E: E.tensor_single_scalar(
                GL[:, 0:NE], GL[:, 0:NE], GL[:, NE:NE + 1], ALU.subtract),
                reads=["glA", "glB"], writes=["glA"])
            b.op("scalar", lambda E: E.activation(
                GL[:, NE + 2:2 * NE + 2], GL[:, 0:NE], AF.Exp),
                reads=["glA"], writes=["glD"])
            b.op("vector", lambda E: E.tensor_tensor(
                GL[:, NE + 2:2 * NE + 2], GL[:, NE + 2:2 * NE + 2],
                GL[:, 2 * NE + 2:3 * NE + 2], ALU.mult),
                reads=["glD", "glE"], writes=["glD"])
            b.op("vector", lambda E: E.reduce_sum(
                GL[:, NE:NE + 1], GL[:, NE + 2:2 * NE + 2], AX.X),
                reads=["glD"], writes=["glB"])
            b.op("vector", lambda E: E.reciprocal(
                GL[:, NE + 1:NE + 2], GL[:, NE:NE + 1]),
                reads=["glB"], writes=["glC"])
            b.op("vector", lambda E, _rc=rc: E.tensor_single_scalar(
                g_all[:, _rc * NE:(_rc + 1) * NE], GL[:, NE + 2:2 * NE + 2],
                GL[:, NE + 1:NE + 2], ALU.mult),
                reads=["glD", "glC"], writes=[f"g{rc}"])
            b.op("vector", lambda E, _rc=rc: E.tensor_copy(
                gbf[:, :], g_all[:, _rc * NE:(_rc + 1) * NE]),
                reads=[f"g{rc}"], writes=["gbf"])
            b.op("tensor", lambda E: E.transpose(
                pbt[0:NE, 0:128], gbf[:, :], ident[:, :]),
                reads=["gbf", "ident"], writes=["pbt"])
            b.op("vector", lambda E, _rc=rc: E.tensor_copy(
                gt_sb[0:NE, _rc * 128:(_rc + 1) * 128], pbt[0:NE, 0:128]),
                reads=["pbt"], writes=[f"gt{rc}"])

        for dh in range(NDH):
            rc_lo, rc_hi = dh * NRH, (dh + 1) * NRH
            accr = accr2[dh % 2]
            for rc in range(rc_lo, rc_hi):
                ar = rc - rc_lo
                for half in range(NHO):
                    b.op("tensor", lambda E, _rc=rc, _h=half: E.matmul(
                        pf[6][:, 0:HW],
                        gt_sb[0:NE, _rc * 128:(_rc + 1) * 128],
                        eb_sb[0:NE, _h * HW:(_h + 1) * HW],
                        start=True, stop=True),
                        reads=[f"gt{rc}", "eb_sb"], writes=["pf6"])
                    b.op("vector", lambda E, _a=ar, _h=half, _ac=accr:
                         E.tensor_copy(
                             _ac.ap(_a * D + _h * HW, _a * D + (_h + 1) * HW),
                             pf[6][:, 0:HW]),
                         reads=["pf6"], writes=[accr.k(f"{ar}_{half}")])
            for e in range(NE):
                esl = e % 2
                b.op("sync", lambda E, _e=e, _s=esl: E.dma_start(
                    out=we_r[_s].view().rearrange("p (k o) -> p k o", k=KI),
                    in_=webf_d[_e].rearrange("k p o -> p k o")),
                    reads=[f"webf{e}"], writes=[we_r[esl].k()], dma=True)
                for rc in range(rc_lo, rc_hi):
                    ar = rc - rc_lo
                    pi0 = 2 + 2 * (rc % 2)
                    fns = []
                    for ki in range(KI):
                        for half in range(NHO):
                            fns.append(lambda E, _ki=ki, _rc=rc, _h=half,
                                       _pp=pf[pi0 + half], _s=esl: E.matmul(
                                _pp[:, 0:HW],
                                xnT_dh.ap(_ki * SO + _rc * 128,
                                          _ki * SO + (_rc + 1) * 128),
                                we_r[_s].ap(_ki * D + _h * HW,
                                            _ki * D + (_h + 1) * HW),
                                start=(_ki == 0), stop=(_ki == KI - 1)))
                    b.group("tensor", fns,
                            reads=xnd_keys + [we_r[esl].k()],
                            writes=[f"pf{pi0}", f"pf{pi0 + 1}"])
                    for half in range(NHO):
                        kk = f"pf{pi0 + half}"
                        b.op("vector", lambda E, _a=ar, _h=half, _rc=rc, _e=e,
                             _pp=pf[pi0 + half], _ac=accr:
                             E.scalar_tensor_tensor(
                                 _ac.ap(_a * D + _h * HW,
                                        _a * D + (_h + 1) * HW),
                                 _pp[:, 0:HW],
                                 g_all[:, _rc * NE + _e:_rc * NE + _e + 1],
                                 _ac.ap(_a * D + _h * HW,
                                        _a * D + (_h + 1) * HW),
                                 ALU.mult, ALU.add),
                             reads=[kk, f"g{rc}", accr.k(f"{ar}_{half}")],
                             writes=[accr.k(f"{ar}_{half}")])
            for rc in range(rc_lo, rc_hi):
                ar = rc - rc_lo
                sl = rc % 2
                b.op("sync", lambda E, _sl=sl, _rc=rc: E.dma_start(
                    out=srow[_sl].view(),
                    in_=sdram[_rc * 128:(_rc + 1) * 128, :]),
                    reads=[f"sd{rc}"], writes=[srow[sl].k()], dma=True)
                b.op("sync", lambda E, _sl=sl, _rc=rc: E.dma_start(
                    out=xrow[_sl].view(),
                    in_=x_d[_rc * 128:(_rc + 1) * 128, :]),
                    writes=[xrow[sl].k()], dma=True)
                b.op("vector", lambda E, _sl=sl, _rc=rc: E.tensor_scalar(
                    xnrow.view(), xrow[_sl].view(),
                    stats_rs[:, _rc:_rc + 1], stats_nb[:, _rc:_rc + 1],
                    ALU.mult, ALU.add),
                    reads=[xrow[sl].k(), f"rs{rc}", f"nb{rc}"],
                    writes=[xnrow.k()])
                b.op("scalar", lambda E, _a=ar, _ac=accr: E.activation(
                    gsil.view(), _ac.ap(_a * D, (_a + 1) * D), AF.Silu),
                    reads=[accr.k(f"{ar}_{h}") for h in range(NHO)],
                    writes=[gsil.k()])
                b.op("vector", lambda E, _sl=sl: E.tensor_tensor(
                    srow[_sl].view(), srow[_sl].view(), xnrow.view(),
                    ALU.add),
                    reads=[srow[sl].k(), xnrow.k()], writes=[srow[sl].k()])
                b.op("gpsimd", lambda E, _sl=sl: E.tensor_tensor(
                    gsil.view(), srow[_sl].view(), gsil.view(), ALU.mult),
                    reads=[srow[sl].k(), gsil.k()], writes=[gsil.k()])
                b.op("gpsimd", lambda E, _sl=sl: E.tensor_tensor(
                    srow[_sl].view(), gsil.view(), xrow[_sl].view(), ALU.add),
                    reads=[gsil.k(), xrow[sl].k()], writes=[srow[sl].k()])
                b.op("sync", lambda E, _o=sl, _rc=rc: E.dma_start(
                    out=out_d[_rc * 128:(_rc + 1) * 128, :],
                    in_=srow[_o].view()),
                    reads=[srow[sl].k()], writes=[f"out{rc}"], dma=True)

        b.schedule_and_emit()

        with nc.Block() as block:
            for eng in ENGINES:
                getattr(block, eng)(
                    lambda E, _eng=eng: [f(E) for f in b.cmds[_eng]])

    return nc



def _perm_blocks(cfg, parity):
    RC = cfg["RC"]
    return list(range(parity, RC, 2)) + list(range(1 - parity, RC, 2))


def _rope_tables(cfg, perm):
    S, DH = cfg["S"], 64
    inv = 1.0 / (10000.0 ** (np.arange(0, DH, 2, dtype=np.float64) / DH))
    t = np.arange(S, dtype=np.float64)
    fr = np.outer(t, inv)
    emb = np.concatenate([fr, fr], axis=1)
    cos = (np.cos(emb) / 8.0).astype(np.float32)
    sin = (np.sin(emb) / 8.0).astype(np.float32)
    rowperm = np.concatenate([np.arange(b * 128, (b + 1) * 128) for b in perm])
    cos = cos[rowperm]
    sin = sin[rowperm]
    cosT = np.ascontiguousarray(np.concatenate([cos.T, cos.T], axis=0))
    sinT = np.ascontiguousarray(np.concatenate([sin.T, sin.T], axis=0))
    return cosT, sinT


def _consts(cfg, parity):
    QG = cfg["QG"]
    nqb = QG // 128
    ident = np.eye(128, dtype=np.float32).astype(BF)
    bones = np.zeros((128, 128), np.float32)
    bones[0:64, 0:64] = 1.0
    bones[64:128, 64:128] = 1.0
    bones = bones.astype(BF)
    R = np.zeros((64, 64), np.float32)
    for i in range(0, 64, 2):
        R[i, i + 1] = -1.0
        R[i + 1, i] = 1.0
    RT = np.zeros((128, 128), np.float32)
    RT[0:64, 0:64] = R.T
    RT[64:128, 64:128] = R.T
    masks = np.zeros((2, nqb, 128, QG), np.float32)
    tri = np.tril(np.ones((128, 128), np.float32)).T
    for p in range(nqb):
        for j in range(nqb):
            blk = slice(j * 128, (j + 1) * 128)
            if j > p:
                masks[0][p][:, blk] = 1.0
            elif j == p:
                masks[0][p][:, blk] = tri
            if (p < j) or (parity == 1 and p == j):
                masks[1][p][:, blk] = 1.0
    return ident, bones, np.ascontiguousarray(RT).astype(BF), masks.astype(BF)


def _core_inputs(cfg, core, ins):
    S, D, KI, C2 = cfg["S"], cfg["D"], cfg["KI"], cfg["C2"]
    NE = cfg["NE"]
    b_idx, parity = core // 2, core % 2
    f32 = np.float32
    perm = _perm_blocks(cfg, parity)
    rowperm = np.concatenate([np.arange(b * 128, (b + 1) * 128) for b in perm])
    x = np.ascontiguousarray(
        np.asarray(ins["L0_attn_x"], dtype=f32)[b_idx][rowperm])

    def wT(W):
        t = np.ascontiguousarray(np.asarray(W, dtype=f32).T)
        return np.ascontiguousarray(t.reshape(KI, 128, D))

    wq = wT(ins["L0_attn_Wq"])
    wk = wT(ins["L0_attn_Wk"])
    wv = wT(ins["L0_attn_Wv"])
    wo = np.ascontiguousarray(
        np.asarray(ins["L0_attn_Wo"], dtype=f32).T.reshape(C2, 128, D))
    ew = np.asarray(ins["L0_ffn_expert_w"], dtype=f32)
    we = np.ascontiguousarray(ew.reshape(NE, KI, 128, D))
    eb = np.ascontiguousarray(np.asarray(ins["L0_ffn_expert_b"], dtype=f32))
    gw = np.ascontiguousarray(
        np.asarray(ins["L0_ffn_gate_w"], dtype=f32).T.reshape(KI, 128, NE))
    gb = np.ascontiguousarray(np.broadcast_to(
        np.asarray(ins["L0_ffn_gate_b"], dtype=f32)[None, :], (128, NE)))
    bq = np.asarray(ins["L0_attn_bq"], dtype=f32)
    bk = np.asarray(ins["L0_attn_bk"], dtype=f32)
    bqT = np.ascontiguousarray(bq.reshape(C2, 128).T)
    bkT = np.ascontiguousarray(bk.reshape(C2, 128).T)
    bv = np.ascontiguousarray(np.broadcast_to(
        np.asarray(ins["L0_attn_bv"], dtype=f32)[None, :], (128, D)))
    nw = np.asarray(ins["L0_attn_norm_w"], dtype=f32)
    nw128 = np.ascontiguousarray(np.tile(nw, 2)[:, None])
    pidx = np.arange(64) ^ 1
    nwp128 = np.ascontiguousarray(np.tile(nw[pidx], 2)[:, None])
    cosT, sinT = _rope_tables(cfg, perm)
    ident, bones, RT, masks = _consts(cfg, parity)
    return {
        "x": x, "wq": wq, "wk": wk, "wv": wv, "wo": wo, "we": we,
        "gw": gw, "gb": gb, "eb": eb, "bqT": bqT, "bkT": bkT, "bv_bc": bv,
        "cosw": cosT, "sinw": sinT, "nw128": nw128, "nwp128": nwp128,
        "identbf": ident, "bonesbf": bones, "rotT": RT, "masks_h": masks,
    }


def _assemble(cfg, outs):
    S, D = cfg["S"], cfg["D"]
    B = cfg["NCORES"] // 2
    full = np.empty((B, S, D), np.float32)
    for b_idx in range(B):
        for parity in range(2):
            o = outs[2 * b_idx + parity]
            for i, blk in enumerate(range(parity, cfg["RC"], 2)):
                full[b_idx, blk * 128:(blk + 1) * 128] = \
                    o[i * 128:(i + 1) * 128]
    return full


def kernel(**inputs):
    cfg = make_cfg(S=4096, D=1024, NH=16, NE=8)
    nc = build_program(cfg)
    in_maps = [_core_inputs(cfg, c, inputs) for c in range(8)]
    res = run_bass_kernel_spmd(nc, in_maps, core_ids=list(range(8)))
    outs = [np.asarray(res.results[c]["out"]) for c in range(8)]
    return _assemble(cfg, outs).astype(np.float32)


# version 24
# speedup vs baseline: 1.0372x; 1.0372x over previous
import numpy as np
import ml_dtypes
import concourse.bass as bass
import concourse.mybir as mybir
from concourse.bass_utils import run_bass_kernel_spmd
from contextlib import ExitStack

F32 = mybir.dt.float32
BF16 = mybir.dt.bfloat16
AF = mybir.ActivationFunctionType
ALU = mybir.AluOpType
AX = mybir.AxisListType

BF = ml_dtypes.bfloat16
EPS_LN = 1e-5
EPS_RMS = float(np.finfo(np.float32).eps)
ENGINES = ("tensor", "vector", "scalar", "gpsimd", "sync")


def make_cfg(S=4096, D=1024, NH=16, NE=8):
    DH = 64
    assert D == NH * DH
    cfg = dict(S=S, D=D, NH=NH, DH=DH, NE=NE)
    cfg["C2"] = NH // 2
    cfg["KI"] = D // 128
    cfg["RC"] = S // 128
    cfg["SO"] = S // 2
    cfg["RCO"] = S // 256
    cfg["QG"] = min(512, cfg["SO"])
    cfg["NQG"] = cfg["SO"] // cfg["QG"]
    cfg["AQ"] = min(512, S)
    cfg["NAP"] = S // cfg["AQ"]
    assert cfg["SO"] % 512 == 0, "own rows must be a multiple of 512"
    cfg["NCORES"] = 8
    return cfg



class Node:
    __slots__ = ("idx", "eng", "fns", "dma", "deps", "cost", "tick",
                 "nwaits")

    def __init__(self, idx, eng, fns, dma, cost):
        self.idx, self.eng, self.fns, self.dma = idx, eng, fns, dma
        self.deps = set()
        self.cost = cost
        self.tick = None


_COST = {"tensor": 400, "vector": 700, "scalar": 750, "gpsimd": 1400,
         "sync": 1500}


class Builder:

    def __init__(self, nc):
        self.nc = nc
        self.cmds = {e: [] for e in ENGINES}
        self.dma_pools = {"ds": [f"ds{i}" for i in range(8)],
                          "dg": [f"dg{i}" for i in range(4)]}
        self.semnames = (["T", "V", "S", "G"]
                         + sum(self.dma_pools.values(), []))
        self.sems = {}
        self.nodes = []
        self.keys = {}

    def seed_key(self, key, pending):
        rec = self.keys.setdefault(key, {"w": None, "r": []})
        rec["r"].extend(pending)

    def key_join(self, keys):
        out = []
        for k in keys:
            rec = self.keys.get(k)
            if not rec:
                continue
            if rec["w"]:
                out.append(rec["w"])
            out.extend(rec["r"])
        return out

    def group(self, eng, fns, reads=(), writes=(), dma=False, cost=None):
        n = Node(len(self.nodes), eng, fns, dma,
                 cost or _COST[eng] * len(fns))
        self.nodes.append(n)
        for k in reads:
            rec = self.keys.setdefault(k, {"w": None, "r": []})
            if rec["w"] is not None:
                n.deps.add(rec["w"].idx)
            rec["r"].append(n)
        for k in writes:
            rec = self.keys.setdefault(k, {"w": None, "r": []})
            if rec["w"] is not None:
                n.deps.add(rec["w"].idx)
            for r in rec["r"]:
                if r is not n:
                    n.deps.add(r.idx)
            self.keys[k] = {"w": n, "r": []}
        return n

    def op(self, eng, fn, reads=(), writes=(), dma=False, cost=None):
        return self.group(eng, [fn], reads=reads, writes=writes, dma=dma,
                          cost=cost)

    def schedule_and_emit(self):
        import heapq
        nodes = self.nodes
        nn = len(nodes)
        children = [[] for _ in range(nn)]
        npend = [0] * nn
        for v in nodes:
            for d in v.deps:
                children[d].append(v.idx)
            npend[v.idx] = len(v.deps)
        finish = [0.0] * nn
        eng_avail = {e: 0.0 for e in ENGINES}
        ready = []
        for v in nodes:
            if npend[v.idx] == 0:
                heapq.heappush(ready, (0.0, v.idx))
        order = []
        est_ready = [0.0] * nn
        while ready:
            _, i = heapq.heappop(ready)
            v = nodes[i]
            start = max(est_ready[i], eng_avail[v.eng])
            fin = start + v.cost
            finish[i] = fin
            eng_avail[v.eng] = fin
            order.append(i)
            for c in children[i]:
                est_ready[c] = max(est_ready[c], fin)
                npend[c] -= 1
                if npend[c] == 0:
                    heapq.heappush(ready, (est_ready[c], c))
        assert len(order) == nn, "cycle in op graph"
        cnt = {sname: 0 for sname in self.semnames}
        rr = {"ds": 0, "dg": 0}
        for i in order:
            v = nodes[i]
            if v.dma:
                pool = "dg" if v.eng == "gpsimd" else "ds"
                names = self.dma_pools[pool]
                sname = names[rr[pool] % len(names)]
                rr[pool] += 1
                prev = cnt[sname]
                cnt[sname] += 16 * len(v.fns)
                v.tick = (sname, cnt[sname], prev)
            else:
                sname = {"tensor": "T", "vector": "V", "scalar": "S",
                         "gpsimd": "G"}[v.eng]
                cnt[sname] += 1
                v.tick = (sname, cnt[sname], None)
        waited = {e: {s: 0 for s in self.semnames} for e in ENGINES}
        for i in order:
            v = nodes[i]
            need = {}
            for d in v.deps:
                s, val, _ = nodes[d].tick
                need[s] = max(need.get(s, 0), val)
            if v.dma and v.tick[2]:
                s = v.tick[0]
                need[s] = max(need.get(s, 0), v.tick[2])
            for s, val in sorted(need.items()):
                if waited[v.eng][s] < val:
                    waited[v.eng][s] = val
                    self.cmds[v.eng].append(
                        lambda E, _s=self.sems[s], _v=val: E.wait_ge(_s, _v))
            sem = self.sems[v.tick[0]]
            step = 16 if v.dma else 1
            nf = len(v.fns)
            for j, fn in enumerate(v.fns):
                def run(E, _fn=fn, _last=(j == nf - 1 or v.dma), _sem=sem,
                        _step=step):
                    inst = _fn(E)
                    if _last:
                        inst.then_inc(_sem, _step)
                self.cmds[v.eng].append(run)
        for eng, pool in (("sync", "ds"), ("gpsimd", "dg")):
            for sname in self.dma_pools[pool]:
                val = cnt[sname]
                if val > 0 and waited[eng][sname] < val:
                    self.cmds[eng].append(
                        lambda E, _s=self.sems[sname], _v=val:
                        E.wait_ge(_s, _v))


class Region:
    def __init__(self, arena, name, lo, hi):
        self.arena, self.name, self.lo, self.hi = arena, name, lo, hi
        self.keys = set()
        self.pending = {}

    def ap(self, lo, hi, plo=0, phi=128):
        assert self.lo + hi <= self.hi, f"{self.name}: {hi} > {self.hi - self.lo}"
        return self.arena.t[plo:phi, self.lo + lo:self.lo + hi]

    def view(self):
        return self.arena.t[:, self.lo:self.hi]

    def k(self, sub=""):
        key = f"{self.name}/{sub}"
        if key not in self.keys:
            self.keys.add(key)
            self.arena.b.seed_key(key, self.pending)
        return key


class Arena:
    def __init__(self, b, tensor, ncols):
        self.b, self.t, self.n = b, tensor, ncols
        self.free_iv = [(0, ncols)]
        self.live = {}
        self.dead = []
        self.peak = 0

    def alloc(self, name, cols):
        for i, (lo, hi) in enumerate(self.free_iv):
            if hi - lo >= cols:
                r = Region(self, name, lo, lo + cols)
                if hi - lo == cols:
                    self.free_iv.pop(i)
                else:
                    self.free_iv[i] = (lo + cols, hi)
                pend = []
                keep = []
                for (dlo, dhi, dp) in self.dead:
                    if dlo < r.hi and dhi > r.lo:
                        pend.extend(dp)
                        if dlo < r.lo:
                            keep.append((dlo, r.lo, dp))
                        if dhi > r.hi:
                            keep.append((r.hi, dhi, dp))
                    else:
                        keep.append((dlo, dhi, dp))
                self.dead = keep
                r.pending = pend
                self.live[name] = r
                used = self.n - sum(h - l for l, h in self.free_iv)
                self.peak = max(self.peak, used)
                return r
        raise RuntimeError(f"arena OOM for {name} ({cols} cols); "
                           f"free={self.free_iv}")

    def free(self, name):
        r = self.live.pop(name)
        pend = self.b.key_join(r.keys)
        self.dead.append((r.lo, r.hi, pend))
        self.free_iv.append((r.lo, r.hi))
        self.free_iv.sort()
        merged = []
        for lo, hi in self.free_iv:
            if merged and merged[-1][1] == lo:
                merged[-1] = (merged[-1][0], hi)
            else:
                merged.append((lo, hi))
        self.free_iv = merged



def build_program(cfg):
    nc = bass.Bass("TRN2", target_bir_lowering=False)
    S, D, KI, C2, RC = cfg["S"], cfg["D"], cfg["KI"], cfg["C2"], cfg["RC"]
    SO, RCO = cfg["SO"], cfg["RCO"]
    QG, NQG, AQ, NAP = cfg["QG"], cfg["NQG"], cfg["AQ"], cfg["NAP"]
    NE = cfg["NE"]
    HW = min(512, D)
    NHO = D // HW
    NQB = QG // 128

    dt = nc.dram_tensor
    x_d = dt("x", [S, D], F32, kind="ExternalInput")
    wq_d = dt("wq", [KI, 128, D], F32, kind="ExternalInput")
    wk_d = dt("wk", [KI, 128, D], F32, kind="ExternalInput")
    wv_d = dt("wv", [KI, 128, D], F32, kind="ExternalInput")
    wo_d = dt("wo", [C2, 128, D], F32, kind="ExternalInput")
    we_d = dt("we", [NE, KI, 128, D], F32, kind="ExternalInput")
    gw_d = dt("gw", [KI, 128, NE], F32, kind="ExternalInput")
    gb_d = dt("gb", [128, NE], F32, kind="ExternalInput")
    eb_d = dt("eb", [NE, D], F32, kind="ExternalInput")
    bqT_d = dt("bqT", [128, C2], F32, kind="ExternalInput")
    bkT_d = dt("bkT", [128, C2], F32, kind="ExternalInput")
    bv_d = dt("bv_bc", [128, D], F32, kind="ExternalInput")
    cosw_d = dt("cosw", [128, S], F32, kind="ExternalInput")
    sinw_d = dt("sinw", [128, S], F32, kind="ExternalInput")
    nw_d = dt("nw128", [128, 1], F32, kind="ExternalInput")
    nwp_d = dt("nwp128", [128, 1], F32, kind="ExternalInput")
    id_d = dt("identbf", [128, 128], BF16, kind="ExternalInput")
    bo_d = dt("bonesbf", [128, 128], BF16, kind="ExternalInput")
    rt_d = dt("rotT", [128, 128], BF16, kind="ExternalInput")
    mk_d = dt("masks_h", [2, NQB, 128, QG], BF16, kind="ExternalInput")
    qsp_d = dt("qsp", [C2, 128, SO], BF16)
    ksp_d = dt("ksp", [C2, 128, S], BF16)
    vsp_d = dt("vsp", [RC, 128, D], BF16)
    zsp_d = dt("zsp", [C2, 128, SO], BF16)
    xnsp_d = dt("xnsp", [128, KI, S], BF16)
    sdram = dt("sdram", [SO, D], F32)
    webf_d = dt("webf", [NE, KI, 128, D], BF16)
    out_d = dt("out", [SO, D], F32, kind="ExternalOutput")

    es = ExitStack()
    sbt = lambda name, shape, dtp: es.enter_context(
        nc.sbuf_tensor("s_" + name, shape, dtp))
    pst = lambda name, shape, dtp: es.enter_context(
        nc.psum_tensor(name, shape, dtp))

    with es:
        ident = sbt("ident", [128, 128], BF16)
        bones = sbt("bones", [128, 128], BF16)
        rotT = sbt("rott", [128, 128], BF16)
        masks = sbt("masks", [128, 2 * NQB * QG], BF16)
        stats_rs = sbt("stats_rs", [128, RC], F32)
        stats_nb = sbt("stats_nb", [128, RC], F32)
        bqT = sbt("bqT", [128, C2], F32)
        bkT = sbt("bkT", [128, C2], F32)
        bv_bc = sbt("bv_bc", [128, D], F32)
        gb_bc = sbt("gb_bc", [128, NE], F32)
        nw128 = sbt("nw128", [128, 1], F32)
        nwp128 = sbt("nwp128", [128, 1], F32)
        gw_sb = sbt("gw_sb", [128, KI * NE], BF16)
        eb_sb = sbt("eb_sb", [128, D], BF16)
        g_all = sbt("g_all", [128, RCO * NE], F32)
        gbf = sbt("gbf", [128, NE], BF16)
        gt_sb = sbt("gt_sb", [128, SO], BF16)
        tiny = sbt("tiny", [128, 16], F32)
        cst = sbt("cst", [128, 4], F32)
        gltmp = sbt("gltmp", [128, 4 * NE], F32)

        need_a_bf = (3 * KI * D + 2 * KI * AQ + 2 * D + 12 * 512
                     + 2 * C2 * 512 + 2 * D)
        need_b_bf = (2 * (2 * S + RC * 256) + 2 * C2 * 512 + 8 * QG
                     + C2 * D + 2 * QG + 2 * C2 * 128)
        NDH = min(4, RCO)
        NRH = RCO // NDH
        need_d_bf = (2 * KI * D + KI * SO)
        NBF = max(need_a_bf, need_b_bf, need_d_bf) + 6144
        need_a_f = 4 * AQ + 4 * D + 16 * 512
        need_de_f = 2 * D + 2 * NRH * D + 6 * D
        NF32 = max(need_a_f, need_de_f) + 256
        abf_t = sbt("abf", [128, NBF], BF16)
        af_t = sbt("af32", [128, NF32], F32)

        class PView:
            def __init__(self, t, off):
                self.t, self.off = t, off
            def __getitem__(self, idx):
                ps, cs = idx
                a = cs.start or 0
                bnd = cs.stop if cs.stop is not None else 512
                return self.t[ps, self.off + a:self.off + bnd]

        p01 = pst("p01", [128, 1024], F32)
        p23 = pst("p23", [128, 1024], F32)
        p45 = pst("p45", [128, 1024], F32)
        p6 = pst("p6", [128, 512], F32)
        pbt = pst("pbt", [128, 1024], BF16)
        pf = [PView(p01, 0), PView(p01, 512), PView(p23, 0), PView(p23, 512),
              PView(p45, 0), PView(p45, 512), PView(p6, 0)]
        pfull = [p01, p23, p45]

        b = Builder(nc)
        for sn in b.semnames:
            b.sems[sn] = es.enter_context(nc.semaphore(sn))
        ABF = Arena(b, abf_t, NBF)
        AF32 = Arena(b, af_t, NF32)

        def ld(eng, dst, src, key):
            b.op(eng, lambda E, _d=dst, _s=src: E.dma_start(out=_d, in_=_s),
                 writes=[key], dma=True)

        ld("sync", ident[:, :], id_d[:, :], "ident")
        ld("sync", bones[:, :], bo_d[:, :], "bones")
        ld("sync", rotT[:, :], rt_d[:, :], "rotT")
        ld("sync", masks[:, :].rearrange("p (c m q) -> p c m q", c=2, m=NQB),
           mk_d.rearrange("c m p q -> p c m q"), "masks")
        ld("sync", bqT[:, :], bqT_d[:, :], "bqT")
        ld("sync", bkT[:, :], bkT_d[:, :], "bkT")
        ld("sync", bv_bc[:, :], bv_d[:, :], "bv_bc")
        ld("sync", gb_bc[:, :], gb_d[:, :], "gb_bc")
        ld("sync", nw128[:, :], nw_d[:, :], "nw128")
        ld("sync", nwp128[:, :], nwp_d[:, :], "nwp128")
        ld("gpsimd", gw_sb[:, :].rearrange("p (k e) -> p k e", k=KI),
           gw_d.rearrange("k p e -> p k e"), "gw_sb")
        ld("gpsimd", eb_sb[0:NE, :], eb_d[:, :], "eb_sb")
        b.op("vector", lambda E: E.memset(cst[:, 0:1], EPS_LN), writes=["cst0"])
        b.op("vector", lambda E: E.memset(cst[:, 1:2], EPS_RMS), writes=["cst1"])
        b.op("vector", lambda E: E.memset(cst[:, 2:3], 1.0 / 64), writes=["cst2"])
        b.op("vector", lambda E: E.memset(cst[:, 3:4], -0.5), writes=["cst3"])

        wq_r = ABF.alloc("wq", KI * D)
        wk_r = ABF.alloc("wk", KI * D)
        wv_r = ABF.alloc("wv", KI * D)
        xnT_q2 = [ABF.alloc(f"xnT_q{i}", KI * AQ) for i in range(2)]
        xnbf = [ABF.alloc(f"xnbf{i}", D) for i in range(2)]
        sqb = [ABF.alloc(f"sqb{i}", 512) for i in range(4)]
        qstage = ABF.alloc("qstage", C2 * 512)
        kstage = ABF.alloc("kstage", C2 * 512)
        vstage = [ABF.alloc(f"vstage{i}", D) for i in range(2)]
        cosw2 = [AF32.alloc(f"cosw{i}", AQ) for i in range(2)]
        sinw2 = [AF32.alloc(f"sinw{i}", AQ) for i in range(2)]
        xbuf = [AF32.alloc(f"xbuf{i}", D) for i in range(2)]
        xsq2 = [AF32.alloc(f"xsq{i}", D) for i in range(2)]
        srtb = [AF32.alloc(f"srtb{i}", 512) for i in range(4)]
        rstdb = [AF32.alloc(f"rstdb{i}", 512) for i in range(4)]
        qnb = [ABF.alloc(f"qnb{i}", 512) for i in range(4)]
        t1b = [AF32.alloc(f"t1b{i}", 512) for i in range(4)]
        t2b = [AF32.alloc(f"t2b{i}", 512) for i in range(4)]

        for wrg, wd in ((wq_r, wq_d), (wk_r, wk_d), (wv_r, wv_d)):
            ld("gpsimd", wrg.view().rearrange("p (k o) -> p k o", k=KI),
               wd.rearrange("k p o -> p k o"), wrg.k())

        def qk_tile(wreg, bcol, bkey, c, rb, grb, sl, isq, xnT_q, cosw, sinw,
                    psl):
            pp, kpp = pf[psl % 4], f"pf{psl % 4}"
            vb, kvb = pf[4 + psl % 2], f"pf{4 + psl % 2}"
            xnt_keys = [xnT_q.k(f"r{rr}") for rr in
                        range(rb // 128, rb // 128 + 4)]
            ops = []
            fns = []
            for ki in range(KI):
                fns.append(lambda E, _ki=ki, _c=c, _rb=rb, _w=wreg, _pp=pp,
                           _x=xnT_q: E.matmul(
                    _pp[:, :],
                    _w.ap(_ki * D + _c * 128, _ki * D + (_c + 1) * 128),
                    _x.ap(_ki * AQ + _rb, _ki * AQ + _rb + 512),
                    start=(_ki == 0), stop=(_ki == KI - 1)))
            ops.append(("tensor", fns, [wreg.k()] + xnt_keys, [kpp]))
            ops.append(("scalar", [lambda E, _sl=sl, _pp=pp, _b=bcol, _c=c:
                 E.activation(sqb[_sl].view(), _pp[:, :], AF.Square,
                              bias=_b[:, _c:_c + 1])],
                 [kpp, bkey], [sqb[sl].k()]))
            ops.append([("tensor", [lambda E, _sl=sl, _vb=vb: E.matmul(
                _vb[:, :], bones[:, :], sqb[_sl].view(),
                start=True, stop=True)],
                [sqb[sl].k(), "bones"], [kvb]),
                ("scalar", [lambda E, _sl=sl, _vb=vb: E.activation(
                    srtb[_sl].view(), _vb[:, :], AF.Ln,
                    bias=cst[:, 1:2], scale=cst[:, 2:3])],
                    [kvb, "cst1", "cst2"], [srtb[sl].k()])])
            ops.append(("scalar", [lambda E, _sl=sl: E.activation(
                rstdb[_sl].view(), srtb[_sl].view(), AF.Exp,
                scale=cst[:, 3:4])],
                [srtb[sl].k(), "cst3"], [rstdb[sl].k()]))
            ops.append(("vector", [lambda E, _sl=sl, _pp=pp, _b=bcol, _c=c:
                 E.scalar_tensor_tensor(
                     qnb[_sl].view(), _pp[:, :], _b[:, _c:_c + 1],
                     rstdb[_sl].view(), ALU.add, ALU.mult)],
                 [kpp, rstdb[sl].k(), bkey], [qnb[sl].k()]))
            ops.append(("tensor", [lambda E, _sl=sl, _pp=pp: E.matmul(
                _pp[:, :], rotT[:, :], qnb[_sl].view(),
                start=True, stop=True)],
                [qnb[sl].k(), "rotT"], [kpp]))
            ops.append(("gpsimd", [lambda E, _sl=sl, _rb=rb, _cw=cosw:
                 E.tensor_tensor(
                     t1b[_sl].view(), qnb[_sl].view(), _cw.ap(_rb, _rb + 512),
                     ALU.mult)],
                 [qnb[sl].k(), cosw.k()], [t1b[sl].k()]))
            ops.append(("vector", [lambda E, _sl=sl, _rb=rb, _sw=sinw,
                 _pp=pp: E.tensor_tensor(
                     t2b[_sl].view(), _pp[:, :], _sw.ap(_rb, _rb + 512),
                     ALU.mult)],
                 [kpp, sinw.k()], [t2b[sl].k()]))
            stg, skey = (qstage, qstage.k(f"c{c}")) if isq else                 (kstage, kstage.k(f"c{c}"))
            ops.append(("gpsimd", [lambda E, _sl=sl, _c=c, _st=stg:
                 E.tensor_tensor(
                     _st.ap(_c * 512, (_c + 1) * 512),
                     t1b[_sl].view(), t2b[_sl].view(), ALU.add)],
                 [t1b[sl].k(), t2b[sl].k()], [skey]))
            return ops

        def emit_interleaved(tile_ops, width=4):
            for base in range(0, len(tile_ops), width):
                grp = tile_ops[base:base + width]
                nst = max(len(t) for t in grp)
                for st in range(nst):
                    for t in grp:
                        if st < len(t):
                            specs = t[st]
                            if isinstance(specs, tuple):
                                specs = [specs]
                            for spec in specs:
                                if len(spec) == 5:
                                    eng, fns, rd, wr, dma = spec
                                else:
                                    eng, fns, rd, wr = spec
                                    dma = False
                                b.group(eng, fns, reads=rd, writes=wr,
                                        dma=dma)

        def fuse_proj(tiles):
            for i in range(0, len(tiles) - 1, 2):
                ea, fa, ra, wa = tiles[i][0]
                eb, fb, rbb, wb = tiles[i + 1][0]
                fz = [f for pair in zip(fa, fb) for f in pair]
                tiles[i][0] = (ea, fz, list(dict.fromkeys(ra + rbb)),
                               wa + wb)
                tiles[i + 1][0] = []

        tctr = [0]
        for ap_i in range(NAP):
            rowbase = ap_i * AQ
            asl = ap_i % 2
            xnT_q = xnT_q2[asl]
            cosw, sinw = cosw2[asl], sinw2[asl]
            b.op("sync", lambda E, _r=rowbase, _c=cosw: E.dma_start(
                out=_c.view(), in_=cosw_d[:, _r:_r + AQ]),
                writes=[cosw.k()], dma=True)
            b.op("sync", lambda E, _r=rowbase, _s=sinw: E.dma_start(
                out=_s.view(), in_=sinw_d[:, _r:_r + AQ]),
                writes=[sinw.k()], dma=True)
            b.op("vector", lambda E, _c=cosw: E.tensor_scalar_mul(
                _c.view(), _c.view(), nw128[:, 0:1]),
                reads=[cosw.k(), "nw128"], writes=[cosw.k()])
            b.op("vector", lambda E, _s=sinw: E.tensor_scalar_mul(
                _s.view(), _s.view(), nwp128[:, 0:1]),
                reads=[sinw.k(), "nwp128"], writes=[sinw.k()])

            def ln_tile(r, rc, xnT_q):
                sl = r % 2
                tb = 8 * sl
                kx = xbuf[sl].k()
                xsq = xsq2[sl]
                T = lambda j: tiny[:, tb + j:tb + j + 1]
                ops = []
                ops.append(("sync", [lambda E, _rc=rc, _sl=sl: E.dma_start(
                    out=xbuf[_sl].view(),
                    in_=x_d[_rc * 128:(_rc + 1) * 128, :])],
                    [], [kx], True))
                ops.append([
                    ("vector", [lambda E, _sl=sl: E.reduce_sum(
                        T(0), xbuf[_sl].view(), AX.X)], [kx], [f"t0_{sl}"]),
                    ("scalar", [lambda E, _sl=sl, _x=xsq: E.activation(
                        _x.view(), xbuf[_sl].view(), AF.Square)],
                        [kx], [xsq.k()])])
                ops.append(("vector", [lambda E, _x=xsq: E.reduce_sum(
                    T(1), _x.view(), AX.X)], [xsq.k()], [f"t1_{sl}"]))
                ops.append([
                    ("vector", [lambda E: E.tensor_scalar_mul(
                        T(2), T(0), 1.0 / D)], [f"t0_{sl}"], [f"t2_{sl}"]),
                    ("vector", [lambda E: E.tensor_tensor(
                        T(3), T(2), T(2), ALU.mult)], [f"t2_{sl}"],
                        [f"t3_{sl}"]),
                    ("vector", [lambda E: E.scalar_tensor_tensor(
                        T(4), T(1), 1.0 / D, T(3), ALU.mult, ALU.subtract)],
                        [f"t1_{sl}", f"t3_{sl}"], [f"t4_{sl}"])])
                ops.append([
                    ("scalar", [lambda E: E.activation(
                        T(5), T(4), AF.Ln, bias=cst[:, 0:1])],
                        [f"t4_{sl}", "cst0"], [f"t5_{sl}"]),
                    ("scalar", [lambda E, _rc=rc: E.activation(
                        stats_rs[:, _rc:_rc + 1], T(5), AF.Exp,
                        scale=cst[:, 3:4])],
                        [f"t5_{sl}", "cst3"], [f"rs{rc}"])])
                ops.append(("vector", [lambda E, _rc=rc: E.scalar_tensor_tensor(
                    stats_nb[:, _rc:_rc + 1], T(2), -1.0,
                    stats_rs[:, _rc:_rc + 1], ALU.mult, ALU.mult)],
                    [f"t2_{sl}", f"rs{rc}"], [f"nb{rc}"]))
                ops.append(("vector", [lambda E, _sl=sl, _rc=rc: E.tensor_scalar(
                    xnbf[_sl].view(), xbuf[_sl].view(),
                    stats_rs[:, _rc:_rc + 1], stats_nb[:, _rc:_rc + 1],
                    ALU.mult, ALU.add)],
                    [kx, f"rs{rc}", f"nb{rc}"], [xnbf[sl].k()]))
                fns = []
                for ki in range(KI):
                    fns.append(lambda E, _sl=sl, _ki=ki: E.transpose(
                        pbt[:, _ki * 128:(_ki + 1) * 128],
                        xnbf[_sl].ap(_ki * 128, (_ki + 1) * 128), ident[:, :]))
                ops.append([
                    ("tensor", fns, [xnbf[sl].k(), "ident"], ["pbt"]),
                    ("vector", [lambda E, _r=r, _x=xnT_q: E.tensor_copy(
                        _x.view().rearrange("p (k n) -> p k n", k=KI)
                        [:, :, _r * 128:(_r + 1) * 128],
                        pbt[:, 0:KI * 128].rearrange("p (k n) -> p k n",
                                                     k=KI))],
                        ["pbt"], [xnT_q.k(f"r{r}")])])
                return ops

            nrc = AQ // 128
            ln_tiles = [ln_tile(r, ap_i * nrc + r, xnT_q) for r in range(nrc)]
            emit_interleaved(ln_tiles, width=2)

            for g in range(AQ // 512):
                rb = g * 512
                grb = rowbase + rb
                tiles = []
                for c in range(C2):
                    tiles.append(qk_tile(wk_r, bkT, "bkT", c, rb, grb,
                                         tctr[0] % 4, False, xnT_q, cosw,
                                         sinw, tctr[0]))
                    tctr[0] += 1
                emit_interleaved(tiles)
                b.op("sync", lambda E, _g=grb: E.dma_start(
                    out=ksp_d[:, :, _g:_g + 512].rearrange("c p r -> p c r"),
                    in_=kstage.view().rearrange("p (c r) -> p c r", c=C2)),
                    reads=[kstage.k(f"c{c}") for c in range(C2)],
                    writes=[f"ksp{grb}"], dma=True)
                if grb < SO:
                    tiles = []
                    for c in range(C2):
                        tiles.append(qk_tile(wq_r, bqT, "bqT", c, rb, grb,
                                             tctr[0] % 4, True, xnT_q, cosw,
                                             sinw, tctr[0]))
                        tctr[0] += 1
                    emit_interleaved(tiles)
                    b.op("sync", lambda E, _g=grb: E.dma_start(
                        out=qsp_d[:, :, _g:_g + 512].rearrange(
                            "c p r -> p c r"),
                        in_=qstage.view().rearrange("p (c r) -> p c r", c=C2)),
                        reads=[qstage.k(f"c{c}") for c in range(C2)],
                        writes=[f"qsp{grb}"], dma=True)
                for r4 in range(4):
                    rc = (grb + r4 * 128) // 128
                    vs = vstage[rc % 2]
                    for half in range(NHO):
                        pi = 6
                        fns = []
                        for ki in range(KI):
                            fns.append(lambda E, _ki=ki, _rb=rb, _r4=r4,
                                       _h=half, _pi=pi, _x=xnT_q: E.matmul(
                                pf[_pi][:, 0:HW],
                                _x.ap(_ki * AQ + _rb + _r4 * 128,
                                      _ki * AQ + _rb + _r4 * 128 + 128),
                                wv_r.ap(_ki * D + _h * HW,
                                        _ki * D + (_h + 1) * HW),
                                start=(_ki == 0), stop=(_ki == KI - 1)))
                        b.group("tensor", fns,
                                reads=[wv_r.k(), xnT_q.k(f"r{rb // 128 + r4}")],
                                writes=[f"pf{pi}"])
                        b.op("vector", lambda E, _h=half, _pi=pi, _vs=vs:
                             E.tensor_tensor(
                                 _vs.ap(_h * HW, (_h + 1) * HW),
                                 pf[_pi][:, 0:HW],
                                 bv_bc[:, _h * HW:(_h + 1) * HW], ALU.add),
                             reads=[f"pf{pi}", "bv_bc"],
                             writes=[vs.k(str(half))])
                    b.op("sync", lambda E, _rc=rc, _vs=vs: E.dma_start(
                        out=vsp_d[_rc, :, :], in_=_vs.view()),
                        reads=[vs.k(str(h)) for h in range(NHO)],
                        writes=[f"vsp{rc}"], dma=True)
            for ki in range(KI):
                b.op("sync", lambda E, _ki=ki, _rb=rowbase, _x=xnT_q:
                     E.dma_start(
                         out=xnsp_d[:, _ki, _rb:_rb + AQ],
                         in_=_x.ap(_ki * AQ, (_ki + 1) * AQ)),
                     reads=[xnT_q.k(f"r{rr}") for rr in range(AQ // 128)],
                     writes=[f"xnsp{ki}_{ap_i}"], dma=True)

        for nm in (["wq", "wk", "wv", "xnT_q0", "xnT_q1", "xnbf0", "xnbf1",
                    "qstage", "kstage", "vstage0", "vstage1"]
                   + [f"sqb{i}" for i in range(4)]
                   + [f"qnb{i}" for i in range(4)]):
            ABF.free(nm)
        for nm in (["cosw0", "cosw1", "sinw0", "sinw1", "xbuf0", "xbuf1",
                    "xsq0", "xsq1"]
                   + [f"{p}{i}" for p in ("srtb", "rstdb", "t1b",
                                          "t2b") for i in range(4)]):
            AF32.free(nm)

        qT_sl = [ABF.alloc(f"qTsl{i}", QG) for i in range(2)]
        kT_w = [ABF.alloc(f"kTw{i}", S) for i in range(2)]
        v_w = [ABF.alloc(f"vw{i}", RC * 128) for i in range(2)]
        patt = [ABF.alloc(f"patt{i}", 2 * QG) for i in range(4)]
        zst = [ABF.alloc(f"zst{i}", QG) for i in range(2)]
        woT = ABF.alloc("woT", C2 * D)
        sstage = [AF32.alloc(f"sstage{i}", D) for i in range(2)]

        ld("gpsimd", woT.view().rearrange("p (c o) -> p c o", c=C2),
           wo_d.rearrange("c p o -> p c o"), woT.k())

        evac_rr = [0]
        for w in range(C2):
            wsl = w % 2
            kkw = kT_w[wsl].k()
            kvw = v_w[wsl].k()
            kkw_keys = {g: kkw for g in range(0, S, 512)}
            kvw_keys = {g: kvw for g in range(0, S, 512)}
            b.op("sync", lambda E, _w=w, _s=wsl: E.dma_start(
                out=kT_w[_s].view(), in_=ksp_d[_w, :, :]),
                reads=[f"ksp{g}" for g in range(0, S, 512)],
                writes=[kkw], dma=True)
            b.op("sync", lambda E, _w=w, _s=wsl: E.dma_start(
                out=v_w[_s].view().rearrange("p (r c) -> p r c", r=RC),
                in_=vsp_d[:, :, _w * 128:(_w + 1) * 128].rearrange(
                    "r p c -> p r c")),
                reads=[f"vsp{rc}" for rc in range(RC)],
                writes=[kvw], dma=True)
            for qg in range(NQG):
                qb = qg * QG
                sl = qg % 2
                kqT = qT_sl[sl].k()
                b.op("sync", lambda E, _sl=sl, _qb=qb, _w=w: E.dma_start(
                    out=qT_sl[_sl].ap(0, QG),
                    in_=qsp_d[_w, :, _qb:_qb + QG]),
                    reads=[f"qsp{g}" for g in range(qb, qb + QG, 512)],
                    writes=[kqT], dma=True)
                imax = (qb + QG) // 128 - 1
                klist = ([("own", j) for j in range(imax + 1)] +
                         [("par", j) for j in range(imax + 1)])
                pend = []
                for kn, (cls, j) in enumerate(klist):
                    kb = j if cls == "own" else RCO + j
                    inb = j - qb // 128
                    masked = 0 <= inb < NQB
                    moff = ((0 if cls == "own" else NQB) + inb) * QG
                    sp = pfull[1 + kn % 2]
                    kb0, kb1 = (("pf2", "pf3") if kn % 2 == 0
                                else ("pf4", "pf5"))
                    ring = kn % 4
                    pat = patt[ring]
                    kp = pat.k()
                    b.group("tensor", [
                        (lambda E, _u=u, _kb=kb, _sl=sl,
                         _sp=sp, _ws=wsl: E.matmul(
                             _sp[:, _u * QG:(_u + 1) * QG],
                             kT_w[_ws].ap(_kb * 128, (_kb + 1) * 128,
                                          64 * _u, 64 * _u + 64),
                             qT_sl[_sl].ap(0, QG, 64 * _u, 64 * _u + 64),
                             start=True, stop=True,
                             tile_position=(64 * _u, 0)))
                        for u in range(2)],
                        reads=[kkw_keys[(kb // 4) * 512], kqT],
                        writes=[kb0, kb1])
                    e = evac_rr[0] % 8
                    evac_rr[0] += 1
                    if e in (0, 1, 2, 4, 5, 7):
                        b.op("scalar", lambda E, _sp=sp, _pat=pat:
                             E.activation(_pat.ap(0, 2 * QG),
                                          _sp[:, 0:2 * QG], AF.Square),
                             reads=[kb0, kb1], writes=[kp])
                    elif e == 6:
                        b.op("vector", lambda E, _sp=sp, _pat=pat:
                             E.tensor_copy(_pat.ap(0, 2 * QG),
                                           _sp[:, 0:2 * QG]),
                             reads=[kb0, kb1], writes=[kp])
                        b.op("gpsimd", lambda E, _pat=pat:
                             E.tensor_tensor(_pat.ap(0, 2 * QG),
                                             _pat.ap(0, 2 * QG),
                                             _pat.ap(0, 2 * QG), ALU.mult),
                             reads=[kp], writes=[kp])
                    else:
                        b.op("vector", lambda E, _sp=sp, _pat=pat:
                             E.tensor_copy(_pat.ap(0, 2 * QG),
                                           _sp[:, 0:2 * QG]),
                             reads=[kb0, kb1], writes=[kp])
                        b.op("vector", lambda E, _pat=pat:
                             E.tensor_tensor(_pat.ap(0, 2 * QG),
                                             _pat.ap(0, 2 * QG),
                                             _pat.ap(0, 2 * QG), ALU.mult),
                             reads=[kp], writes=[kp])
                    if masked:
                        b.op("vector", lambda E, _pat=pat, _m=moff:
                             E.tensor_tensor(
                                 _pat.ap(0, 2 * QG).rearrange(
                                     "p (u q) -> p u q", u=2),
                                 _pat.ap(0, 2 * QG).rearrange(
                                     "p (u q) -> p u q", u=2),
                                 masks[:, _m:_m + QG].rearrange(
                                     "p q -> p () q").broadcast_to(
                                         [128, 2, QG]),
                                 ALU.mult),
                             reads=[kp, "masks"], writes=[kp])
                    def emit_z(_kb=kb, _pat=pat, _kp=kp,
                               _first=(kn == 0),
                               _last=(kn == len(klist) - 1), _ws=wsl,
                               _kvw=kvw_keys[(kb // 4) * 512]):
                        b.group("tensor", [
                            (lambda E, _u=u: E.matmul(
                                p01[64 * _u:64 * _u + 64,
                                    _u * QG:(_u + 1) * QG],
                                v_w[_ws].ap(_kb * 128 + _u * 64,
                                            _kb * 128 + _u * 64 + 64),
                                _pat.ap(_u * QG, (_u + 1) * QG),
                                start=_first, stop=_last,
                                tile_position=(0, 64 * _u),
                                skip_group_check=True))
                            for u in range(2)],
                            reads=[_kp, _kvw, "pf0", "pf1"],
                            writes=["pf0", "pf1"])
                    pend.append(emit_z)
                    if len(pend) > 3:
                        pend.pop(0)()
                for f in pend:
                    f()
                zs = zst[qg % 2]
                b.op("vector", lambda E, _zs=zs: E.tensor_copy(
                    _zs.ap(0, QG, 0, 64), p01[0:64, 0:QG]),
                    reads=["pf0"], writes=[zs.k("a")])
                b.op("vector", lambda E, _zs=zs: E.tensor_copy(
                    _zs.ap(0, QG, 64, 128), p01[64:128, QG:2 * QG]),
                    reads=["pf1"], writes=[zs.k("b")])
                b.op("sync", lambda E, _hp=w, _qb=qb, _zs=zs: E.dma_start(
                    out=zsp_d[_hp, :, _qb:_qb + QG], in_=_zs.view()),
                    reads=[zs.k("a"), zs.k("b")],
                    writes=[f"zsp{w}_{qb}"], dma=True)

        for e in range(NE):
            b.op("gpsimd", lambda E, _e=e: E.dma_start(
                out=webf_d[_e], in_=we_d[_e]),
                writes=[f"webf{e}"], dma=True)

        zrc = [ABF.alloc(f"zrc{i}", C2 * 128) for i in range(2)]
        for rc in range(RCO):
            sl = rc % 2
            zr = zrc[sl]
            b.op("sync", lambda E, _rc=rc, _zr=zr: E.dma_start(
                out=_zr.view().rearrange("p (c r) -> p c r", c=C2),
                in_=zsp_d[:, :, _rc * 128:(_rc + 1) * 128].rearrange(
                    "c p r -> p c r")),
                reads=[f"zsp{hp}_{(rc * 128 // QG) * QG}" for hp in range(C2)],
                writes=[zr.k()], dma=True)
            for half in range(NHO):
                pp, kk = pf[2 + half % 2], f"pf{2 + half % 2}"
                fns = []
                for hp in range(C2):
                    fns.append(lambda E, _hp=hp, _rc=rc, _h=half, _pp=pp,
                               _zr=zr: E.matmul(
                        _pp[:, 0:HW],
                        _zr.ap(_hp * 128, (_hp + 1) * 128),
                        woT.ap(_hp * D + _h * HW, _hp * D + (_h + 1) * HW),
                        start=(_hp == 0), stop=(_hp == C2 - 1)))
                b.group("tensor", fns, reads=[zr.k(), woT.k()], writes=[kk])
                if half % 2 == 0:
                    b.op("vector", lambda E, _sl=sl, _pp=pp, _h=half:
                         E.tensor_copy(
                             sstage[_sl].ap(_h * HW, (_h + 1) * HW),
                             _pp[:, 0:HW]),
                         reads=[kk], writes=[sstage[sl].k(str(half))])
                else:
                    b.op("scalar", lambda E, _sl=sl, _pp=pp, _h=half: E.copy(
                        sstage[_sl].ap(_h * HW, (_h + 1) * HW), _pp[:, 0:HW]),
                        reads=[kk], writes=[sstage[sl].k(str(half))])
            b.op("sync", lambda E, _sl=sl, _rc=rc: E.dma_start(
                out=sdram[_rc * 128:(_rc + 1) * 128, :],
                in_=sstage[_sl].view()),
                reads=[sstage[sl].k(str(h)) for h in range(NHO)],
                writes=[f"sd{rc}"], dma=True)

        for nm in ("qTsl0", "qTsl1", "kTw0", "kTw1", "vw0", "vw1", "woT",
                   "zst0", "zst1", "zrc0", "zrc1"):
            ABF.free(nm)
        for nm in [f"patt{i}" for i in range(4)]:
            ABF.free(nm)
        AF32.free("sstage0")
        AF32.free("sstage1")

        we_r = [ABF.alloc(f"we{i}", KI * D) for i in range(2)]
        xnT_dh = ABF.alloc("xnT_dh", KI * SO)
        xnd_keys = []
        for ki in range(KI):
            kxk = xnT_dh.k(f"{ki}")
            xnd_keys.append(kxk)
            b.op("sync", lambda E, _ki=ki: E.dma_start(
                out=xnT_dh.ap(_ki * SO, (_ki + 1) * SO),
                in_=xnsp_d[:, _ki, 0:SO]),
                reads=[f"xnsp{ki}_{a}" for a in range(NAP)],
                writes=[kxk], dma=True)
        accr2 = [AF32.alloc(f"accr{i}", NRH * D) for i in range(2)]
        srow = [AF32.alloc(f"srow{i}", D) for i in range(2)]
        xrow = [AF32.alloc(f"xrow{i}", D) for i in range(2)]
        xnrow = AF32.alloc("xnrow", D)
        gsil = AF32.alloc("gsil", D)

        GL = gltmp
        for rc in range(RCO):
            fns = []
            for ki in range(KI):
                fns.append(lambda E, _ki=ki, _rc=rc: E.matmul(
                    pf[6][:, 0:NE],
                    xnT_dh.ap(_ki * SO + _rc * 128, _ki * SO + (_rc + 1) * 128),
                    gw_sb[:, _ki * NE:(_ki + 1) * NE],
                    start=(_ki == 0), stop=(_ki == KI - 1)))
            b.group("tensor", fns, reads=xnd_keys + ["gw_sb"], writes=["pf6"])
            b.op("vector", lambda E: E.tensor_tensor(
                GL[:, 0:NE], pf[6][:, 0:NE], gb_bc[:, :], ALU.add),
                reads=["pf6", "gb_bc"], writes=["glA"])
            b.op("vector", lambda E: E.reduce_max(
                GL[:, NE:NE + 1], GL[:, 0:NE], AX.X),
                reads=["glA"], writes=["glB"])
            b.op("vector", lambda E: E.tensor_single_scalar(
                GL[:, NE + 2:2 * NE + 2], GL[:, 0:NE], GL[:, NE:NE + 1],
                ALU.is_equal),
                reads=["glA", "glB"], writes=["glD"])
            b.op("vector", lambda E: E.scalar_tensor_tensor(
                GL[:, 2 * NE + 2:3 * NE + 2], GL[:, NE + 2:2 * NE + 2],
                -1e30, GL[:, 0:NE], ALU.mult, ALU.add),
                reads=["glD", "glA"], writes=["glE"])
            b.op("vector", lambda E: E.reduce_max(
                GL[:, NE + 1:NE + 2], GL[:, 2 * NE + 2:3 * NE + 2], AX.X),
                reads=["glE"], writes=["glC"])
            b.op("vector", lambda E: E.tensor_single_scalar(
                GL[:, 2 * NE + 2:3 * NE + 2], GL[:, 0:NE],
                GL[:, NE + 1:NE + 2], ALU.is_ge),
                reads=["glA", "glC"], writes=["glE"])
            b.op("vector", lambda E: E.tensor_single_scalar(
                GL[:, 0:NE], GL[:, 0:NE], GL[:, NE:NE + 1], ALU.subtract),
                reads=["glA", "glB"], writes=["glA"])
            b.op("scalar", lambda E: E.activation(
                GL[:, NE + 2:2 * NE + 2], GL[:, 0:NE], AF.Exp),
                reads=["glA"], writes=["glD"])
            b.op("vector", lambda E: E.tensor_tensor(
                GL[:, NE + 2:2 * NE + 2], GL[:, NE + 2:2 * NE + 2],
                GL[:, 2 * NE + 2:3 * NE + 2], ALU.mult),
                reads=["glD", "glE"], writes=["glD"])
            b.op("vector", lambda E: E.reduce_sum(
                GL[:, NE:NE + 1], GL[:, NE + 2:2 * NE + 2], AX.X),
                reads=["glD"], writes=["glB"])
            b.op("vector", lambda E: E.reciprocal(
                GL[:, NE + 1:NE + 2], GL[:, NE:NE + 1]),
                reads=["glB"], writes=["glC"])
            b.op("vector", lambda E, _rc=rc: E.tensor_single_scalar(
                g_all[:, _rc * NE:(_rc + 1) * NE], GL[:, NE + 2:2 * NE + 2],
                GL[:, NE + 1:NE + 2], ALU.mult),
                reads=["glD", "glC"], writes=[f"g{rc}"])
            b.op("vector", lambda E, _rc=rc: E.tensor_copy(
                gbf[:, :], g_all[:, _rc * NE:(_rc + 1) * NE]),
                reads=[f"g{rc}"], writes=["gbf"])
            b.op("tensor", lambda E: E.transpose(
                pbt[0:NE, 0:128], gbf[:, :], ident[:, :]),
                reads=["gbf", "ident"], writes=["pbt"])
            b.op("vector", lambda E, _rc=rc: E.tensor_copy(
                gt_sb[0:NE, _rc * 128:(_rc + 1) * 128], pbt[0:NE, 0:128]),
                reads=["pbt"], writes=[f"gt{rc}"])

        we_loaded = [None, None]
        for dh in range(NDH):
            rc_lo, rc_hi = dh * NRH, (dh + 1) * NRH
            accr = accr2[dh % 2]
            for rc in range(rc_lo, rc_hi):
                ar = rc - rc_lo
                for half in range(NHO):
                    b.op("tensor", lambda E, _rc=rc, _h=half: E.matmul(
                        pf[6][:, 0:HW],
                        gt_sb[0:NE, _rc * 128:(_rc + 1) * 128],
                        eb_sb[0:NE, _h * HW:(_h + 1) * HW],
                        start=True, stop=True),
                        reads=[f"gt{rc}", "eb_sb"], writes=["pf6"])
                    b.op("vector", lambda E, _a=ar, _h=half, _ac=accr:
                         E.tensor_copy(
                             _ac.ap(_a * D + _h * HW, _a * D + (_h + 1) * HW),
                             pf[6][:, 0:HW]),
                         reads=["pf6"], writes=[accr.k(f"{ar}_{half}")])
            eorder = list(range(NE)) if dh % 2 == 0 else \
                list(range(NE - 1, -1, -1))
            for e in eorder:
                esl = e % 2
                if we_loaded[esl] != e:
                    KQ = KI // 4
                    b.group("sync", [
                        (lambda E, _e=e, _s=esl, _k0=k0: E.dma_start(
                            out=we_r[_s].view().rearrange(
                                "p (k o) -> p k o", k=KI)[:, _k0:_k0 + KQ],
                            in_=webf_d[_e][_k0:_k0 + KQ].rearrange(
                                "k p o -> p k o")))
                        for k0 in range(0, KI, KQ)],
                        reads=[f"webf{e}"], writes=[we_r[esl].k()], dma=True)
                    we_loaded[esl] = e
                for rc in range(rc_lo, rc_hi):
                    ar = rc - rc_lo
                    pi0 = 2 + 2 * (rc % 2)
                    fns = []
                    for ki in range(KI):
                        for half in range(NHO):
                            fns.append(lambda E, _ki=ki, _rc=rc, _h=half,
                                       _pp=pf[pi0 + half], _s=esl: E.matmul(
                                _pp[:, 0:HW],
                                xnT_dh.ap(_ki * SO + _rc * 128,
                                          _ki * SO + (_rc + 1) * 128),
                                we_r[_s].ap(_ki * D + _h * HW,
                                            _ki * D + (_h + 1) * HW),
                                start=(_ki == 0), stop=(_ki == KI - 1)))
                    b.group("tensor", fns,
                            reads=xnd_keys + [we_r[esl].k()],
                            writes=[f"pf{pi0}", f"pf{pi0 + 1}"])
                    for half in range(NHO):
                        kk = f"pf{pi0 + half}"
                        b.op("vector", lambda E, _a=ar, _h=half, _rc=rc, _e=e,
                             _pp=pf[pi0 + half], _ac=accr:
                             E.scalar_tensor_tensor(
                                 _ac.ap(_a * D + _h * HW,
                                        _a * D + (_h + 1) * HW),
                                 _pp[:, 0:HW],
                                 g_all[:, _rc * NE + _e:_rc * NE + _e + 1],
                                 _ac.ap(_a * D + _h * HW,
                                        _a * D + (_h + 1) * HW),
                                 ALU.mult, ALU.add),
                             reads=[kk, f"g{rc}", accr.k(f"{ar}_{half}")],
                             writes=[accr.k(f"{ar}_{half}")])
            for rc in range(rc_lo, rc_hi):
                ar = rc - rc_lo
                sl = rc % 2
                b.op("sync", lambda E, _sl=sl, _rc=rc: E.dma_start(
                    out=srow[_sl].view(),
                    in_=sdram[_rc * 128:(_rc + 1) * 128, :]),
                    reads=[f"sd{rc}"], writes=[srow[sl].k()], dma=True)
                b.op("sync", lambda E, _sl=sl, _rc=rc: E.dma_start(
                    out=xrow[_sl].view(),
                    in_=x_d[_rc * 128:(_rc + 1) * 128, :]),
                    writes=[xrow[sl].k()], dma=True)
                b.op("vector", lambda E, _sl=sl, _rc=rc: E.tensor_scalar(
                    xnrow.view(), xrow[_sl].view(),
                    stats_rs[:, _rc:_rc + 1], stats_nb[:, _rc:_rc + 1],
                    ALU.mult, ALU.add),
                    reads=[xrow[sl].k(), f"rs{rc}", f"nb{rc}"],
                    writes=[xnrow.k()])
                b.op("scalar", lambda E, _a=ar, _ac=accr: E.activation(
                    gsil.view(), _ac.ap(_a * D, (_a + 1) * D), AF.Silu),
                    reads=[accr.k(f"{ar}_{h}") for h in range(NHO)],
                    writes=[gsil.k()])
                b.op("vector", lambda E, _sl=sl: E.tensor_tensor(
                    srow[_sl].view(), srow[_sl].view(), xnrow.view(),
                    ALU.add),
                    reads=[srow[sl].k(), xnrow.k()], writes=[srow[sl].k()])
                b.op("gpsimd", lambda E, _sl=sl: E.tensor_tensor(
                    gsil.view(), srow[_sl].view(), gsil.view(), ALU.mult),
                    reads=[srow[sl].k(), gsil.k()], writes=[gsil.k()])
                b.op("gpsimd", lambda E, _sl=sl: E.tensor_tensor(
                    srow[_sl].view(), gsil.view(), xrow[_sl].view(), ALU.add),
                    reads=[gsil.k(), xrow[sl].k()], writes=[srow[sl].k()])
                b.op("sync", lambda E, _o=sl, _rc=rc: E.dma_start(
                    out=out_d[_rc * 128:(_rc + 1) * 128, :],
                    in_=srow[_o].view()),
                    reads=[srow[sl].k()], writes=[f"out{rc}"], dma=True)

        b.schedule_and_emit()

        with nc.Block() as block:
            for eng in ENGINES:
                getattr(block, eng)(
                    lambda E, _eng=eng: [f(E) for f in b.cmds[_eng]])

    return nc



def _perm_blocks(cfg, parity):
    RC = cfg["RC"]
    return list(range(parity, RC, 2)) + list(range(1 - parity, RC, 2))


def _rope_tables(cfg, perm):
    S, DH = cfg["S"], 64
    inv = 1.0 / (10000.0 ** (np.arange(0, DH, 2, dtype=np.float64) / DH))
    t = np.arange(S, dtype=np.float64)
    fr = np.outer(t, inv)
    emb = np.concatenate([fr, fr], axis=1)
    cos = (np.cos(emb) / 8.0).astype(np.float32)
    sin = (np.sin(emb) / 8.0).astype(np.float32)
    rowperm = np.concatenate([np.arange(b * 128, (b + 1) * 128) for b in perm])
    cos = cos[rowperm]
    sin = sin[rowperm]
    cosT = np.ascontiguousarray(np.concatenate([cos.T, cos.T], axis=0))
    sinT = np.ascontiguousarray(np.concatenate([sin.T, sin.T], axis=0))
    return cosT, sinT


def _consts(cfg, parity):
    QG = cfg["QG"]
    nqb = QG // 128
    ident = np.eye(128, dtype=np.float32).astype(BF)
    bones = np.zeros((128, 128), np.float32)
    bones[0:64, 0:64] = 1.0
    bones[64:128, 64:128] = 1.0
    bones = bones.astype(BF)
    R = np.zeros((64, 64), np.float32)
    for i in range(0, 64, 2):
        R[i, i + 1] = -1.0
        R[i + 1, i] = 1.0
    RT = np.zeros((128, 128), np.float32)
    RT[0:64, 0:64] = R.T
    RT[64:128, 64:128] = R.T
    masks = np.zeros((2, nqb, 128, QG), np.float32)
    tri = np.tril(np.ones((128, 128), np.float32)).T
    for p in range(nqb):
        for j in range(nqb):
            blk = slice(j * 128, (j + 1) * 128)
            if j > p:
                masks[0][p][:, blk] = 1.0
            elif j == p:
                masks[0][p][:, blk] = tri
            if (p < j) or (parity == 1 and p == j):
                masks[1][p][:, blk] = 1.0
    return ident, bones, np.ascontiguousarray(RT).astype(BF), masks.astype(BF)


def _core_inputs(cfg, core, ins):
    S, D, KI, C2 = cfg["S"], cfg["D"], cfg["KI"], cfg["C2"]
    NE = cfg["NE"]
    b_idx, parity = core // 2, core % 2
    f32 = np.float32
    perm = _perm_blocks(cfg, parity)
    rowperm = np.concatenate([np.arange(b * 128, (b + 1) * 128) for b in perm])
    x = np.ascontiguousarray(
        np.asarray(ins["L0_attn_x"], dtype=f32)[b_idx][rowperm])

    def wT(W):
        t = np.ascontiguousarray(np.asarray(W, dtype=f32).T)
        return np.ascontiguousarray(t.reshape(KI, 128, D))

    wq = wT(ins["L0_attn_Wq"])
    wk = wT(ins["L0_attn_Wk"])
    wv = wT(ins["L0_attn_Wv"])
    wo = np.ascontiguousarray(
        np.asarray(ins["L0_attn_Wo"], dtype=f32).T.reshape(C2, 128, D))
    ew = np.asarray(ins["L0_ffn_expert_w"], dtype=f32)
    we = np.ascontiguousarray(ew.reshape(NE, KI, 128, D))
    eb = np.ascontiguousarray(np.asarray(ins["L0_ffn_expert_b"], dtype=f32))
    gw = np.ascontiguousarray(
        np.asarray(ins["L0_ffn_gate_w"], dtype=f32).T.reshape(KI, 128, NE))
    gb = np.ascontiguousarray(np.broadcast_to(
        np.asarray(ins["L0_ffn_gate_b"], dtype=f32)[None, :], (128, NE)))
    bq = np.asarray(ins["L0_attn_bq"], dtype=f32)
    bk = np.asarray(ins["L0_attn_bk"], dtype=f32)
    bqT = np.ascontiguousarray(bq.reshape(C2, 128).T)
    bkT = np.ascontiguousarray(bk.reshape(C2, 128).T)
    bv = np.ascontiguousarray(np.broadcast_to(
        np.asarray(ins["L0_attn_bv"], dtype=f32)[None, :], (128, D)))
    nw = np.asarray(ins["L0_attn_norm_w"], dtype=f32)
    nw128 = np.ascontiguousarray(np.tile(nw, 2)[:, None])
    pidx = np.arange(64) ^ 1
    nwp128 = np.ascontiguousarray(np.tile(nw[pidx], 2)[:, None])
    cosT, sinT = _rope_tables(cfg, perm)
    ident, bones, RT, masks = _consts(cfg, parity)
    return {
        "x": x, "wq": wq, "wk": wk, "wv": wv, "wo": wo, "we": we,
        "gw": gw, "gb": gb, "eb": eb, "bqT": bqT, "bkT": bkT, "bv_bc": bv,
        "cosw": cosT, "sinw": sinT, "nw128": nw128, "nwp128": nwp128,
        "identbf": ident, "bonesbf": bones, "rotT": RT, "masks_h": masks,
    }


def _assemble(cfg, outs):
    S, D = cfg["S"], cfg["D"]
    B = cfg["NCORES"] // 2
    full = np.empty((B, S, D), np.float32)
    for b_idx in range(B):
        for parity in range(2):
            o = outs[2 * b_idx + parity]
            for i, blk in enumerate(range(parity, cfg["RC"], 2)):
                full[b_idx, blk * 128:(blk + 1) * 128] = \
                    o[i * 128:(i + 1) * 128]
    return full


def kernel(**inputs):
    cfg = make_cfg(S=4096, D=1024, NH=16, NE=8)
    nc = build_program(cfg)
    in_maps = [_core_inputs(cfg, c, inputs) for c in range(8)]
    res = run_bass_kernel_spmd(nc, in_maps, core_ids=list(range(8)))
    outs = [np.asarray(res.results[c]["out"]) for c in range(8)]
    return _assemble(cfg, outs).astype(np.float32)


# version 25 (final)
# speedup vs baseline: 1.0466x; 1.0091x over previous
import numpy as np
import ml_dtypes
import concourse.bass as bass
import concourse.mybir as mybir
from concourse.bass_utils import run_bass_kernel_spmd
from contextlib import ExitStack

F32 = mybir.dt.float32
BF16 = mybir.dt.bfloat16
AF = mybir.ActivationFunctionType
ALU = mybir.AluOpType
AX = mybir.AxisListType

BF = ml_dtypes.bfloat16
EPS_LN = 1e-5
EPS_RMS = float(np.finfo(np.float32).eps)
ENGINES = ("tensor", "vector", "scalar", "gpsimd", "sync")


def make_cfg(S=4096, D=1024, NH=16, NE=8):
    DH = 64
    assert D == NH * DH
    cfg = dict(S=S, D=D, NH=NH, DH=DH, NE=NE)
    cfg["C2"] = NH // 2
    cfg["KI"] = D // 128
    cfg["RC"] = S // 128
    cfg["SO"] = S // 2
    cfg["RCO"] = S // 256
    cfg["QG"] = min(512, cfg["SO"])
    cfg["NQG"] = cfg["SO"] // cfg["QG"]
    cfg["AQ"] = min(512, S)
    cfg["NAP"] = S // cfg["AQ"]
    assert cfg["SO"] % 512 == 0, "own rows must be a multiple of 512"
    cfg["NCORES"] = 8
    return cfg



class Node:
    __slots__ = ("idx", "eng", "fns", "dma", "deps", "cost", "tick",
                 "nwaits")

    def __init__(self, idx, eng, fns, dma, cost):
        self.idx, self.eng, self.fns, self.dma = idx, eng, fns, dma
        self.deps = set()
        self.cost = cost
        self.tick = None


_COST = {"tensor": 400, "vector": 700, "scalar": 750, "gpsimd": 1400,
         "sync": 1500}


class Builder:

    def __init__(self, nc):
        self.nc = nc
        self.cmds = {e: [] for e in ENGINES}
        self.dma_pools = {"ds": [f"ds{i}" for i in range(8)],
                          "dg": [f"dg{i}" for i in range(4)]}
        self.semnames = (["T", "V", "S", "G"]
                         + sum(self.dma_pools.values(), []))
        self.sems = {}
        self.nodes = []
        self.keys = {}

    def seed_key(self, key, pending):
        rec = self.keys.setdefault(key, {"w": None, "r": []})
        rec["r"].extend(pending)

    def key_join(self, keys):
        out = []
        for k in keys:
            rec = self.keys.get(k)
            if not rec:
                continue
            if rec["w"]:
                out.append(rec["w"])
            out.extend(rec["r"])
        return out

    def group(self, eng, fns, reads=(), writes=(), dma=False, cost=None):
        n = Node(len(self.nodes), eng, fns, dma,
                 cost or _COST[eng] * len(fns))
        self.nodes.append(n)
        for k in reads:
            rec = self.keys.setdefault(k, {"w": None, "r": []})
            if rec["w"] is not None:
                n.deps.add(rec["w"].idx)
            rec["r"].append(n)
        for k in writes:
            rec = self.keys.setdefault(k, {"w": None, "r": []})
            if rec["w"] is not None:
                n.deps.add(rec["w"].idx)
            for r in rec["r"]:
                if r is not n:
                    n.deps.add(r.idx)
            self.keys[k] = {"w": n, "r": []}
        return n

    def op(self, eng, fn, reads=(), writes=(), dma=False, cost=None):
        return self.group(eng, [fn], reads=reads, writes=writes, dma=dma,
                          cost=cost)

    def schedule_and_emit(self):
        import heapq
        nodes = self.nodes
        nn = len(nodes)
        children = [[] for _ in range(nn)]
        npend = [0] * nn
        for v in nodes:
            for d in v.deps:
                children[d].append(v.idx)
            npend[v.idx] = len(v.deps)
        finish = [0.0] * nn
        eng_avail = {e: 0.0 for e in ENGINES}
        ready = []
        for v in nodes:
            if npend[v.idx] == 0:
                heapq.heappush(ready, (0.0, v.idx))
        order = []
        est_ready = [0.0] * nn
        while ready:
            _, i = heapq.heappop(ready)
            v = nodes[i]
            start = max(est_ready[i], eng_avail[v.eng])
            fin = start + v.cost
            finish[i] = fin
            eng_avail[v.eng] = fin
            order.append(i)
            for c in children[i]:
                est_ready[c] = max(est_ready[c], fin)
                npend[c] -= 1
                if npend[c] == 0:
                    heapq.heappush(ready, (est_ready[c], c))
        assert len(order) == nn, "cycle in op graph"
        cnt = {sname: 0 for sname in self.semnames}
        rr = {"ds": 0, "dg": 0}
        for i in order:
            v = nodes[i]
            if v.dma:
                pool = "dg" if v.eng == "gpsimd" else "ds"
                names = self.dma_pools[pool]
                sname = names[rr[pool] % len(names)]
                rr[pool] += 1
                prev = cnt[sname]
                cnt[sname] += 16 * len(v.fns)
                v.tick = (sname, cnt[sname], prev)
            else:
                sname = {"tensor": "T", "vector": "V", "scalar": "S",
                         "gpsimd": "G"}[v.eng]
                cnt[sname] += 1
                v.tick = (sname, cnt[sname], None)
        waited = {e: {s: 0 for s in self.semnames} for e in ENGINES}
        for i in order:
            v = nodes[i]
            need = {}
            for d in v.deps:
                s, val, _ = nodes[d].tick
                need[s] = max(need.get(s, 0), val)
            if v.dma and v.tick[2]:
                s = v.tick[0]
                need[s] = max(need.get(s, 0), v.tick[2])
            for s, val in sorted(need.items()):
                if waited[v.eng][s] < val:
                    waited[v.eng][s] = val
                    self.cmds[v.eng].append(
                        lambda E, _s=self.sems[s], _v=val: E.wait_ge(_s, _v))
            sem = self.sems[v.tick[0]]
            step = 16 if v.dma else 1
            nf = len(v.fns)
            for j, fn in enumerate(v.fns):
                def run(E, _fn=fn, _last=(j == nf - 1 or v.dma), _sem=sem,
                        _step=step):
                    inst = _fn(E)
                    if _last:
                        inst.then_inc(_sem, _step)
                self.cmds[v.eng].append(run)
        for eng, pool in (("sync", "ds"), ("gpsimd", "dg")):
            for sname in self.dma_pools[pool]:
                val = cnt[sname]
                if val > 0 and waited[eng][sname] < val:
                    self.cmds[eng].append(
                        lambda E, _s=self.sems[sname], _v=val:
                        E.wait_ge(_s, _v))


class Region:
    def __init__(self, arena, name, lo, hi):
        self.arena, self.name, self.lo, self.hi = arena, name, lo, hi
        self.keys = set()
        self.pending = {}

    def ap(self, lo, hi, plo=0, phi=128):
        assert self.lo + hi <= self.hi, f"{self.name}: {hi} > {self.hi - self.lo}"
        return self.arena.t[plo:phi, self.lo + lo:self.lo + hi]

    def view(self):
        return self.arena.t[:, self.lo:self.hi]

    def k(self, sub=""):
        key = f"{self.name}/{sub}"
        if key not in self.keys:
            self.keys.add(key)
            self.arena.b.seed_key(key, self.pending)
        return key


class Arena:
    def __init__(self, b, tensor, ncols):
        self.b, self.t, self.n = b, tensor, ncols
        self.free_iv = [(0, ncols)]
        self.live = {}
        self.dead = []
        self.peak = 0

    def alloc(self, name, cols):
        for i, (lo, hi) in enumerate(self.free_iv):
            if hi - lo >= cols:
                r = Region(self, name, lo, lo + cols)
                if hi - lo == cols:
                    self.free_iv.pop(i)
                else:
                    self.free_iv[i] = (lo + cols, hi)
                pend = []
                keep = []
                for (dlo, dhi, dp) in self.dead:
                    if dlo < r.hi and dhi > r.lo:
                        pend.extend(dp)
                        if dlo < r.lo:
                            keep.append((dlo, r.lo, dp))
                        if dhi > r.hi:
                            keep.append((r.hi, dhi, dp))
                    else:
                        keep.append((dlo, dhi, dp))
                self.dead = keep
                r.pending = pend
                self.live[name] = r
                used = self.n - sum(h - l for l, h in self.free_iv)
                self.peak = max(self.peak, used)
                return r
        raise RuntimeError(f"arena OOM for {name} ({cols} cols); "
                           f"free={self.free_iv}")

    def free(self, name):
        r = self.live.pop(name)
        pend = self.b.key_join(r.keys)
        self.dead.append((r.lo, r.hi, pend))
        self.free_iv.append((r.lo, r.hi))
        self.free_iv.sort()
        merged = []
        for lo, hi in self.free_iv:
            if merged and merged[-1][1] == lo:
                merged[-1] = (merged[-1][0], hi)
            else:
                merged.append((lo, hi))
        self.free_iv = merged



def build_program(cfg):
    nc = bass.Bass("TRN2", target_bir_lowering=False)
    S, D, KI, C2, RC = cfg["S"], cfg["D"], cfg["KI"], cfg["C2"], cfg["RC"]
    SO, RCO = cfg["SO"], cfg["RCO"]
    QG, NQG, AQ, NAP = cfg["QG"], cfg["NQG"], cfg["AQ"], cfg["NAP"]
    NE = cfg["NE"]
    HW = min(512, D)
    NHO = D // HW
    NQB = QG // 128

    dt = nc.dram_tensor
    x_d = dt("x", [S, D], F32, kind="ExternalInput")
    wq_d = dt("wq", [KI, 128, D], F32, kind="ExternalInput")
    wk_d = dt("wk", [KI, 128, D], F32, kind="ExternalInput")
    wv_d = dt("wv", [KI, 128, D], F32, kind="ExternalInput")
    wo_d = dt("wo", [C2, 128, D], F32, kind="ExternalInput")
    we_d = dt("we", [NE, KI, 128, D], F32, kind="ExternalInput")
    gw_d = dt("gw", [KI, 128, NE], F32, kind="ExternalInput")
    gb_d = dt("gb", [128, NE], F32, kind="ExternalInput")
    eb_d = dt("eb", [NE, D], F32, kind="ExternalInput")
    bqT_d = dt("bqT", [128, C2], F32, kind="ExternalInput")
    bkT_d = dt("bkT", [128, C2], F32, kind="ExternalInput")
    bv_d = dt("bv_bc", [128, D], F32, kind="ExternalInput")
    cosw_d = dt("cosw", [128, S], F32, kind="ExternalInput")
    sinw_d = dt("sinw", [128, S], F32, kind="ExternalInput")
    nw_d = dt("nw128", [128, 1], F32, kind="ExternalInput")
    nwp_d = dt("nwp128", [128, 1], F32, kind="ExternalInput")
    id_d = dt("identbf", [128, 128], BF16, kind="ExternalInput")
    bo_d = dt("bonesbf", [128, 128], BF16, kind="ExternalInput")
    rt_d = dt("rotT", [128, 128], BF16, kind="ExternalInput")
    mk_d = dt("masks_h", [2, NQB, 128, QG], BF16, kind="ExternalInput")
    qsp_d = dt("qsp", [C2, 128, SO], BF16)
    ksp_d = dt("ksp", [C2, 128, S], BF16)
    vsp_d = dt("vsp", [RC, 128, D], BF16)
    zsp_d = dt("zsp", [C2, 128, SO], BF16)
    xnsp_d = dt("xnsp", [128, KI, S], BF16)
    sdram = dt("sdram", [SO, D], F32)
    webf_d = dt("webf", [NE, KI, 128, D], BF16)
    out_d = dt("out", [SO, D], F32, kind="ExternalOutput")

    es = ExitStack()
    sbt = lambda name, shape, dtp: es.enter_context(
        nc.sbuf_tensor("s_" + name, shape, dtp))
    pst = lambda name, shape, dtp: es.enter_context(
        nc.psum_tensor(name, shape, dtp))

    with es:
        ident = sbt("ident", [128, 128], BF16)
        bones = sbt("bones", [128, 128], BF16)
        rotT = sbt("rott", [128, 128], BF16)
        masks = sbt("masks", [128, 2 * NQB * QG], BF16)
        stats_rs = sbt("stats_rs", [128, RC], F32)
        stats_nb = sbt("stats_nb", [128, RC], F32)
        bqT = sbt("bqT", [128, C2], F32)
        bkT = sbt("bkT", [128, C2], F32)
        bv_bc = sbt("bv_bc", [128, D], F32)
        gb_bc = sbt("gb_bc", [128, NE], F32)
        nw128 = sbt("nw128", [128, 1], F32)
        nwp128 = sbt("nwp128", [128, 1], F32)
        gw_sb = sbt("gw_sb", [128, KI * NE], BF16)
        eb_sb = sbt("eb_sb", [128, D], BF16)
        g_all = sbt("g_all", [128, RCO * NE], F32)
        gbf = sbt("gbf", [128, NE], BF16)
        gt_sb = sbt("gt_sb", [128, SO], BF16)
        tiny = sbt("tiny", [128, 16], F32)
        cst = sbt("cst", [128, 4], F32)
        gltmp = sbt("gltmp", [128, 4 * NE], F32)

        need_a_bf = (3 * KI * D + 2 * KI * AQ + 2 * D + 12 * 512
                     + 2 * C2 * 512 + 2 * D)
        need_b_bf = (2 * (2 * S + RC * 256) + 2 * C2 * 512 + 8 * QG
                     + C2 * D + 2 * QG + 2 * C2 * 128)
        NDH = min(4, RCO)
        NRH = RCO // NDH
        need_d_bf = (2 * KI * D + KI * SO)
        NBF = max(need_a_bf, need_b_bf, need_d_bf) + 6144
        need_a_f = 4 * AQ + 4 * D + 16 * 512
        need_de_f = 2 * D + 2 * NRH * D + 6 * D
        NF32 = max(need_a_f, need_de_f) + 256
        abf_t = sbt("abf", [128, NBF], BF16)
        af_t = sbt("af32", [128, NF32], F32)

        class PView:
            def __init__(self, t, off):
                self.t, self.off = t, off
            def __getitem__(self, idx):
                ps, cs = idx
                a = cs.start or 0
                bnd = cs.stop if cs.stop is not None else 512
                return self.t[ps, self.off + a:self.off + bnd]

        p01 = pst("p01", [128, 1024], F32)
        p23 = pst("p23", [128, 1024], F32)
        p45 = pst("p45", [128, 1024], F32)
        p6 = pst("p6", [128, 512], F32)
        pbt = pst("pbt", [128, 1024], BF16)
        pf = [PView(p01, 0), PView(p01, 512), PView(p23, 0), PView(p23, 512),
              PView(p45, 0), PView(p45, 512), PView(p6, 0)]
        pfull = [p01, p23, p45]

        b = Builder(nc)
        for sn in b.semnames:
            b.sems[sn] = es.enter_context(nc.semaphore(sn))
        ABF = Arena(b, abf_t, NBF)
        AF32 = Arena(b, af_t, NF32)

        def ld(eng, dst, src, key):
            b.op(eng, lambda E, _d=dst, _s=src: E.dma_start(out=_d, in_=_s),
                 writes=[key], dma=True)

        ld("sync", ident[:, :], id_d[:, :], "ident")
        ld("sync", bones[:, :], bo_d[:, :], "bones")
        ld("sync", rotT[:, :], rt_d[:, :], "rotT")
        ld("sync", masks[:, :].rearrange("p (c m q) -> p c m q", c=2, m=NQB),
           mk_d.rearrange("c m p q -> p c m q"), "masks")
        ld("sync", bqT[:, :], bqT_d[:, :], "bqT")
        ld("sync", bkT[:, :], bkT_d[:, :], "bkT")
        ld("sync", bv_bc[:, :], bv_d[:, :], "bv_bc")
        ld("sync", gb_bc[:, :], gb_d[:, :], "gb_bc")
        ld("sync", nw128[:, :], nw_d[:, :], "nw128")
        ld("sync", nwp128[:, :], nwp_d[:, :], "nwp128")
        ld("gpsimd", gw_sb[:, :].rearrange("p (k e) -> p k e", k=KI),
           gw_d.rearrange("k p e -> p k e"), "gw_sb")
        ld("gpsimd", eb_sb[0:NE, :], eb_d[:, :], "eb_sb")
        b.op("vector", lambda E: E.memset(cst[:, 0:1], EPS_LN), writes=["cst0"])
        b.op("vector", lambda E: E.memset(cst[:, 1:2], EPS_RMS), writes=["cst1"])
        b.op("vector", lambda E: E.memset(cst[:, 2:3], 1.0 / 64), writes=["cst2"])
        b.op("vector", lambda E: E.memset(cst[:, 3:4], -0.5), writes=["cst3"])

        wq_r = ABF.alloc("wq", KI * D)
        wk_r = ABF.alloc("wk", KI * D)
        wv_r = ABF.alloc("wv", KI * D)
        xnT_q2 = [ABF.alloc(f"xnT_q{i}", KI * AQ) for i in range(2)]
        xnbf = [ABF.alloc(f"xnbf{i}", D) for i in range(2)]
        sqb = [ABF.alloc(f"sqb{i}", 512) for i in range(4)]
        qstage = ABF.alloc("qstage", C2 * 512)
        kstage = ABF.alloc("kstage", C2 * 512)
        vstage = [ABF.alloc(f"vstage{i}", D) for i in range(2)]
        cosw2 = [AF32.alloc(f"cosw{i}", AQ) for i in range(2)]
        sinw2 = [AF32.alloc(f"sinw{i}", AQ) for i in range(2)]
        xbuf = [AF32.alloc(f"xbuf{i}", D) for i in range(2)]
        xsq2 = [AF32.alloc(f"xsq{i}", D) for i in range(2)]
        srtb = [AF32.alloc(f"srtb{i}", 512) for i in range(4)]
        rstdb = [AF32.alloc(f"rstdb{i}", 512) for i in range(4)]
        qnb = [ABF.alloc(f"qnb{i}", 512) for i in range(4)]
        t1b = [AF32.alloc(f"t1b{i}", 512) for i in range(4)]
        t2b = [AF32.alloc(f"t2b{i}", 512) for i in range(4)]

        for wrg, wd in ((wk_r, wk_d), (wq_r, wq_d), (wv_r, wv_d)):
            ld("gpsimd", wrg.view().rearrange("p (k o) -> p k o", k=KI),
               wd.rearrange("k p o -> p k o"), wrg.k())

        def qk_tile(wreg, bcol, bkey, c, rb, grb, sl, isq, xnT_q, cosw, sinw,
                    psl):
            pp, kpp = pf[psl % 4], f"pf{psl % 4}"
            vb, kvb = pf[4 + psl % 2], f"pf{4 + psl % 2}"
            xnt_keys = [xnT_q.k(f"r{rr}") for rr in
                        range(rb // 128, rb // 128 + 4)]
            ops = []
            fns = []
            for ki in range(KI):
                fns.append(lambda E, _ki=ki, _c=c, _rb=rb, _w=wreg, _pp=pp,
                           _x=xnT_q: E.matmul(
                    _pp[:, :],
                    _w.ap(_ki * D + _c * 128, _ki * D + (_c + 1) * 128),
                    _x.ap(_ki * AQ + _rb, _ki * AQ + _rb + 512),
                    start=(_ki == 0), stop=(_ki == KI - 1)))
            ops.append(("tensor", fns, [wreg.k()] + xnt_keys, [kpp]))
            ops.append(("scalar", [lambda E, _sl=sl, _pp=pp, _b=bcol, _c=c:
                 E.activation(sqb[_sl].view(), _pp[:, :], AF.Square,
                              bias=_b[:, _c:_c + 1])],
                 [kpp, bkey], [sqb[sl].k()]))
            ops.append([("tensor", [lambda E, _sl=sl, _vb=vb: E.matmul(
                _vb[:, :], bones[:, :], sqb[_sl].view(),
                start=True, stop=True)],
                [sqb[sl].k(), "bones"], [kvb]),
                ("scalar", [lambda E, _sl=sl, _vb=vb: E.activation(
                    srtb[_sl].view(), _vb[:, :], AF.Ln,
                    bias=cst[:, 1:2], scale=cst[:, 2:3])],
                    [kvb, "cst1", "cst2"], [srtb[sl].k()])])
            ops.append(("scalar", [lambda E, _sl=sl: E.activation(
                rstdb[_sl].view(), srtb[_sl].view(), AF.Exp,
                scale=cst[:, 3:4])],
                [srtb[sl].k(), "cst3"], [rstdb[sl].k()]))
            ops.append(("vector", [lambda E, _sl=sl, _pp=pp, _b=bcol, _c=c:
                 E.scalar_tensor_tensor(
                     qnb[_sl].view(), _pp[:, :], _b[:, _c:_c + 1],
                     rstdb[_sl].view(), ALU.add, ALU.mult)],
                 [kpp, rstdb[sl].k(), bkey], [qnb[sl].k()]))
            ops.append(("tensor", [lambda E, _sl=sl, _pp=pp: E.matmul(
                _pp[:, :], rotT[:, :], qnb[_sl].view(),
                start=True, stop=True)],
                [qnb[sl].k(), "rotT"], [kpp]))
            ops.append(("gpsimd", [lambda E, _sl=sl, _rb=rb, _cw=cosw:
                 E.tensor_tensor(
                     t1b[_sl].view(), qnb[_sl].view(), _cw.ap(_rb, _rb + 512),
                     ALU.mult)],
                 [qnb[sl].k(), cosw.k()], [t1b[sl].k()]))
            ops.append(("vector", [lambda E, _sl=sl, _rb=rb, _sw=sinw,
                 _pp=pp: E.tensor_tensor(
                     t2b[_sl].view(), _pp[:, :], _sw.ap(_rb, _rb + 512),
                     ALU.mult)],
                 [kpp, sinw.k()], [t2b[sl].k()]))
            stg, skey = (qstage, qstage.k(f"c{c}")) if isq else                 (kstage, kstage.k(f"c{c}"))
            ops.append(("gpsimd", [lambda E, _sl=sl, _c=c, _st=stg:
                 E.tensor_tensor(
                     _st.ap(_c * 512, (_c + 1) * 512),
                     t1b[_sl].view(), t2b[_sl].view(), ALU.add)],
                 [t1b[sl].k(), t2b[sl].k()], [skey]))
            return ops

        def emit_interleaved(tile_ops, width=4):
            for base in range(0, len(tile_ops), width):
                grp = tile_ops[base:base + width]
                nst = max(len(t) for t in grp)
                for st in range(nst):
                    for t in grp:
                        if st < len(t):
                            specs = t[st]
                            if isinstance(specs, tuple):
                                specs = [specs]
                            for spec in specs:
                                if len(spec) == 5:
                                    eng, fns, rd, wr, dma = spec
                                else:
                                    eng, fns, rd, wr = spec
                                    dma = False
                                b.group(eng, fns, reads=rd, writes=wr,
                                        dma=dma)

        def fuse_proj(tiles):
            for i in range(0, len(tiles) - 1, 2):
                ea, fa, ra, wa = tiles[i][0]
                eb, fb, rbb, wb = tiles[i + 1][0]
                fz = [f for pair in zip(fa, fb) for f in pair]
                tiles[i][0] = (ea, fz, list(dict.fromkeys(ra + rbb)),
                               wa + wb)
                tiles[i + 1][0] = []

        tctr = [0]
        for ap_i in range(NAP):
            rowbase = ap_i * AQ
            asl = ap_i % 2
            xnT_q = xnT_q2[asl]
            cosw, sinw = cosw2[asl], sinw2[asl]
            b.op("sync", lambda E, _r=rowbase, _c=cosw: E.dma_start(
                out=_c.view(), in_=cosw_d[:, _r:_r + AQ]),
                writes=[cosw.k()], dma=True)
            b.op("sync", lambda E, _r=rowbase, _s=sinw: E.dma_start(
                out=_s.view(), in_=sinw_d[:, _r:_r + AQ]),
                writes=[sinw.k()], dma=True)
            b.op("vector", lambda E, _c=cosw: E.tensor_scalar_mul(
                _c.view(), _c.view(), nw128[:, 0:1]),
                reads=[cosw.k(), "nw128"], writes=[cosw.k()])
            b.op("vector", lambda E, _s=sinw: E.tensor_scalar_mul(
                _s.view(), _s.view(), nwp128[:, 0:1]),
                reads=[sinw.k(), "nwp128"], writes=[sinw.k()])

            def ln_tile(r, rc, xnT_q):
                sl = r % 2
                tb = 8 * sl
                kx = xbuf[sl].k()
                xsq = xsq2[sl]
                T = lambda j: tiny[:, tb + j:tb + j + 1]
                ops = []
                ops.append(("sync", [lambda E, _rc=rc, _sl=sl: E.dma_start(
                    out=xbuf[_sl].view(),
                    in_=x_d[_rc * 128:(_rc + 1) * 128, :])],
                    [], [kx], True))
                ops.append([
                    ("vector", [lambda E, _sl=sl: E.reduce_sum(
                        T(0), xbuf[_sl].view(), AX.X)], [kx], [f"t0_{sl}"]),
                    ("scalar", [lambda E, _sl=sl, _x=xsq: E.activation(
                        _x.view(), xbuf[_sl].view(), AF.Square)],
                        [kx], [xsq.k()])])
                ops.append(("vector", [lambda E, _x=xsq: E.reduce_sum(
                    T(1), _x.view(), AX.X)], [xsq.k()], [f"t1_{sl}"]))
                ops.append([
                    ("vector", [lambda E: E.tensor_scalar_mul(
                        T(2), T(0), 1.0 / D)], [f"t0_{sl}"], [f"t2_{sl}"]),
                    ("vector", [lambda E: E.tensor_tensor(
                        T(3), T(2), T(2), ALU.mult)], [f"t2_{sl}"],
                        [f"t3_{sl}"]),
                    ("vector", [lambda E: E.scalar_tensor_tensor(
                        T(4), T(1), 1.0 / D, T(3), ALU.mult, ALU.subtract)],
                        [f"t1_{sl}", f"t3_{sl}"], [f"t4_{sl}"])])
                ops.append([
                    ("scalar", [lambda E: E.activation(
                        T(5), T(4), AF.Ln, bias=cst[:, 0:1])],
                        [f"t4_{sl}", "cst0"], [f"t5_{sl}"]),
                    ("scalar", [lambda E, _rc=rc: E.activation(
                        stats_rs[:, _rc:_rc + 1], T(5), AF.Exp,
                        scale=cst[:, 3:4])],
                        [f"t5_{sl}", "cst3"], [f"rs{rc}"])])
                ops.append(("vector", [lambda E, _rc=rc: E.scalar_tensor_tensor(
                    stats_nb[:, _rc:_rc + 1], T(2), -1.0,
                    stats_rs[:, _rc:_rc + 1], ALU.mult, ALU.mult)],
                    [f"t2_{sl}", f"rs{rc}"], [f"nb{rc}"]))
                ops.append(("vector", [lambda E, _sl=sl, _rc=rc: E.tensor_scalar(
                    xnbf[_sl].view(), xbuf[_sl].view(),
                    stats_rs[:, _rc:_rc + 1], stats_nb[:, _rc:_rc + 1],
                    ALU.mult, ALU.add)],
                    [kx, f"rs{rc}", f"nb{rc}"], [xnbf[sl].k()]))
                fns = []
                for ki in range(KI):
                    fns.append(lambda E, _sl=sl, _ki=ki: E.transpose(
                        pbt[:, _ki * 128:(_ki + 1) * 128],
                        xnbf[_sl].ap(_ki * 128, (_ki + 1) * 128), ident[:, :]))
                ops.append([
                    ("tensor", fns, [xnbf[sl].k(), "ident"], ["pbt"]),
                    ("vector", [lambda E, _r=r, _x=xnT_q: E.tensor_copy(
                        _x.view().rearrange("p (k n) -> p k n", k=KI)
                        [:, :, _r * 128:(_r + 1) * 128],
                        pbt[:, 0:KI * 128].rearrange("p (k n) -> p k n",
                                                     k=KI))],
                        ["pbt"], [xnT_q.k(f"r{r}")])])
                return ops

            nrc = AQ // 128
            ln_tiles = [ln_tile(r, ap_i * nrc + r, xnT_q) for r in range(nrc)]
            emit_interleaved(ln_tiles, width=2)

            for g in range(AQ // 512):
                rb = g * 512
                grb = rowbase + rb
                tiles = []
                for c in range(C2):
                    tiles.append(qk_tile(wk_r, bkT, "bkT", c, rb, grb,
                                         tctr[0] % 4, False, xnT_q, cosw,
                                         sinw, tctr[0]))
                    tctr[0] += 1
                emit_interleaved(tiles)
                b.op("sync", lambda E, _g=grb: E.dma_start(
                    out=ksp_d[:, :, _g:_g + 512].rearrange("c p r -> p c r"),
                    in_=kstage.view().rearrange("p (c r) -> p c r", c=C2)),
                    reads=[kstage.k(f"c{c}") for c in range(C2)],
                    writes=[f"ksp{grb}"], dma=True)
                if grb < SO:
                    tiles = []
                    for c in range(C2):
                        tiles.append(qk_tile(wq_r, bqT, "bqT", c, rb, grb,
                                             tctr[0] % 4, True, xnT_q, cosw,
                                             sinw, tctr[0]))
                        tctr[0] += 1
                    emit_interleaved(tiles)
                    b.op("sync", lambda E, _g=grb: E.dma_start(
                        out=qsp_d[:, :, _g:_g + 512].rearrange(
                            "c p r -> p c r"),
                        in_=qstage.view().rearrange("p (c r) -> p c r", c=C2)),
                        reads=[qstage.k(f"c{c}") for c in range(C2)],
                        writes=[f"qsp{grb}"], dma=True)
                for r4 in range(4):
                    rc = (grb + r4 * 128) // 128
                    vs = vstage[rc % 2]
                    for half in range(NHO):
                        pi = 6
                        fns = []
                        for ki in range(KI):
                            fns.append(lambda E, _ki=ki, _rb=rb, _r4=r4,
                                       _h=half, _pi=pi, _x=xnT_q: E.matmul(
                                pf[_pi][:, 0:HW],
                                _x.ap(_ki * AQ + _rb + _r4 * 128,
                                      _ki * AQ + _rb + _r4 * 128 + 128),
                                wv_r.ap(_ki * D + _h * HW,
                                        _ki * D + (_h + 1) * HW),
                                start=(_ki == 0), stop=(_ki == KI - 1)))
                        b.group("tensor", fns,
                                reads=[wv_r.k(), xnT_q.k(f"r{rb // 128 + r4}")],
                                writes=[f"pf{pi}"])
                        b.op("vector", lambda E, _h=half, _pi=pi, _vs=vs:
                             E.tensor_tensor(
                                 _vs.ap(_h * HW, (_h + 1) * HW),
                                 pf[_pi][:, 0:HW],
                                 bv_bc[:, _h * HW:(_h + 1) * HW], ALU.add),
                             reads=[f"pf{pi}", "bv_bc"],
                             writes=[vs.k(str(half))])
                    b.op("sync", lambda E, _rc=rc, _vs=vs: E.dma_start(
                        out=vsp_d[_rc, :, :], in_=_vs.view()),
                        reads=[vs.k(str(h)) for h in range(NHO)],
                        writes=[f"vsp{rc}"], dma=True)
            for ki in range(KI):
                b.op("sync", lambda E, _ki=ki, _rb=rowbase, _x=xnT_q:
                     E.dma_start(
                         out=xnsp_d[:, _ki, _rb:_rb + AQ],
                         in_=_x.ap(_ki * AQ, (_ki + 1) * AQ)),
                     reads=[xnT_q.k(f"r{rr}") for rr in range(AQ // 128)],
                     writes=[f"xnsp{ki}_{ap_i}"], dma=True)

        for nm in (["wq", "wk", "wv", "xnT_q0", "xnT_q1", "xnbf0", "xnbf1",
                    "qstage", "kstage", "vstage0", "vstage1"]
                   + [f"sqb{i}" for i in range(4)]
                   + [f"qnb{i}" for i in range(4)]):
            ABF.free(nm)
        for nm in (["cosw0", "cosw1", "sinw0", "sinw1", "xbuf0", "xbuf1",
                    "xsq0", "xsq1"]
                   + [f"{p}{i}" for p in ("srtb", "rstdb", "t1b",
                                          "t2b") for i in range(4)]):
            AF32.free(nm)

        qT_sl = [ABF.alloc(f"qTsl{i}", QG) for i in range(2)]
        kT_w = [ABF.alloc(f"kTw{i}", S) for i in range(2)]
        v_w = [ABF.alloc(f"vw{i}", RC * 128) for i in range(2)]
        patt = [ABF.alloc(f"patt{i}", 2 * QG) for i in range(4)]
        zst = [ABF.alloc(f"zst{i}", QG) for i in range(2)]
        woT = ABF.alloc("woT", C2 * D)
        sstage = [AF32.alloc(f"sstage{i}", D) for i in range(2)]

        ld("gpsimd", woT.view().rearrange("p (c o) -> p c o", c=C2),
           wo_d.rearrange("c p o -> p c o"), woT.k())

        evac_rr = [0]
        for w in range(C2):
            wsl = w % 2
            kkw_keys = {}
            for g in range(0, S, 512):
                kg = kT_w[wsl].k(f"g{g}")
                kkw_keys[g] = kg
                b.op("sync", lambda E, _w=w, _s=wsl, _g=g: E.dma_start(
                    out=kT_w[_s].ap(_g, _g + 512),
                    in_=ksp_d[_w, :, _g:_g + 512]),
                    reads=[f"ksp{g}"], writes=[kg], dma=True)
            kvw_keys = {}
            for blk in range(RC // 8):
                kv = v_w[wsl].k(f"v{blk}")
                for g in range(blk * 1024, (blk + 1) * 1024, 512):
                    kvw_keys[g] = kv
                b.op("sync", lambda E, _w=w, _s=wsl, _b=blk: E.dma_start(
                    out=v_w[_s].view().rearrange(
                        "p (r c) -> p r c", r=RC)[:, _b * 8:(_b + 1) * 8],
                    in_=vsp_d[_b * 8:(_b + 1) * 8, :,
                              _w * 128:(_w + 1) * 128].rearrange(
                        "r p c -> p r c")),
                    reads=[f"vsp{rc}" for rc in range(blk * 8,
                                                      (blk + 1) * 8)],
                    writes=[kv], dma=True)
            for qg in range(NQG):
                qb = qg * QG
                sl = qg % 2
                kqT = qT_sl[sl].k()
                b.op("sync", lambda E, _sl=sl, _qb=qb, _w=w: E.dma_start(
                    out=qT_sl[_sl].ap(0, QG),
                    in_=qsp_d[_w, :, _qb:_qb + QG]),
                    reads=[f"qsp{g}" for g in range(qb, qb + QG, 512)],
                    writes=[kqT], dma=True)
                imax = (qb + QG) // 128 - 1
                klist = ([("own", j) for j in range(imax + 1)] +
                         [("par", j) for j in range(imax + 1)])
                pend = []
                for kn, (cls, j) in enumerate(klist):
                    kb = j if cls == "own" else RCO + j
                    inb = j - qb // 128
                    masked = 0 <= inb < NQB
                    moff = ((0 if cls == "own" else NQB) + inb) * QG
                    sp = pfull[1 + kn % 2]
                    kb0, kb1 = (("pf2", "pf3") if kn % 2 == 0
                                else ("pf4", "pf5"))
                    ring = kn % 4
                    pat = patt[ring]
                    kp = pat.k()
                    b.group("tensor", [
                        (lambda E, _u=u, _kb=kb, _sl=sl,
                         _sp=sp, _ws=wsl: E.matmul(
                             _sp[:, _u * QG:(_u + 1) * QG],
                             kT_w[_ws].ap(_kb * 128, (_kb + 1) * 128,
                                          64 * _u, 64 * _u + 64),
                             qT_sl[_sl].ap(0, QG, 64 * _u, 64 * _u + 64),
                             start=True, stop=True,
                             tile_position=(64 * _u, 0)))
                        for u in range(2)],
                        reads=[kkw_keys[(kb // 4) * 512], kqT],
                        writes=[kb0, kb1])
                    e = evac_rr[0] % 8
                    evac_rr[0] += 1
                    if e in (0, 1, 2, 4, 5, 7):
                        b.op("scalar", lambda E, _sp=sp, _pat=pat:
                             E.activation(_pat.ap(0, 2 * QG),
                                          _sp[:, 0:2 * QG], AF.Square),
                             reads=[kb0, kb1], writes=[kp])
                    elif e == 6:
                        b.op("vector", lambda E, _sp=sp, _pat=pat:
                             E.tensor_copy(_pat.ap(0, 2 * QG),
                                           _sp[:, 0:2 * QG]),
                             reads=[kb0, kb1], writes=[kp])
                        b.op("gpsimd", lambda E, _pat=pat:
                             E.tensor_tensor(_pat.ap(0, 2 * QG),
                                             _pat.ap(0, 2 * QG),
                                             _pat.ap(0, 2 * QG), ALU.mult),
                             reads=[kp], writes=[kp])
                    else:
                        b.op("vector", lambda E, _sp=sp, _pat=pat:
                             E.tensor_copy(_pat.ap(0, 2 * QG),
                                           _sp[:, 0:2 * QG]),
                             reads=[kb0, kb1], writes=[kp])
                        b.op("vector", lambda E, _pat=pat:
                             E.tensor_tensor(_pat.ap(0, 2 * QG),
                                             _pat.ap(0, 2 * QG),
                                             _pat.ap(0, 2 * QG), ALU.mult),
                             reads=[kp], writes=[kp])
                    if masked:
                        b.op("vector", lambda E, _pat=pat, _m=moff:
                             E.tensor_tensor(
                                 _pat.ap(0, 2 * QG).rearrange(
                                     "p (u q) -> p u q", u=2),
                                 _pat.ap(0, 2 * QG).rearrange(
                                     "p (u q) -> p u q", u=2),
                                 masks[:, _m:_m + QG].rearrange(
                                     "p q -> p () q").broadcast_to(
                                         [128, 2, QG]),
                                 ALU.mult),
                             reads=[kp, "masks"], writes=[kp])
                    def emit_z(_kb=kb, _pat=pat, _kp=kp,
                               _first=(kn == 0),
                               _last=(kn == len(klist) - 1), _ws=wsl,
                               _kvw=kvw_keys[(kb // 4) * 512]):
                        b.group("tensor", [
                            (lambda E, _u=u: E.matmul(
                                p01[64 * _u:64 * _u + 64,
                                    _u * QG:(_u + 1) * QG],
                                v_w[_ws].ap(_kb * 128 + _u * 64,
                                            _kb * 128 + _u * 64 + 64),
                                _pat.ap(_u * QG, (_u + 1) * QG),
                                start=_first, stop=_last,
                                tile_position=(0, 64 * _u),
                                skip_group_check=True))
                            for u in range(2)],
                            reads=[_kp, _kvw, "pf0", "pf1"],
                            writes=["pf0", "pf1"])
                    pend.append(emit_z)
                    if len(pend) > 3:
                        pend.pop(0)()
                for f in pend:
                    f()
                zs = zst[qg % 2]
                b.op("vector", lambda E, _zs=zs: E.tensor_copy(
                    _zs.ap(0, QG, 0, 64), p01[0:64, 0:QG]),
                    reads=["pf0"], writes=[zs.k("a")])
                b.op("vector", lambda E, _zs=zs: E.tensor_copy(
                    _zs.ap(0, QG, 64, 128), p01[64:128, QG:2 * QG]),
                    reads=["pf1"], writes=[zs.k("b")])
                b.op("sync", lambda E, _hp=w, _qb=qb, _zs=zs: E.dma_start(
                    out=zsp_d[_hp, :, _qb:_qb + QG], in_=_zs.view()),
                    reads=[zs.k("a"), zs.k("b")],
                    writes=[f"zsp{w}_{qb}"], dma=True)

        for e in range(NE):
            b.op("gpsimd", lambda E, _e=e: E.dma_start(
                out=webf_d[_e], in_=we_d[_e]),
                writes=[f"webf{e}"], dma=True)

        zrc = [ABF.alloc(f"zrc{i}", C2 * 128) for i in range(2)]
        for rc in range(RCO):
            sl = rc % 2
            zr = zrc[sl]
            b.op("sync", lambda E, _rc=rc, _zr=zr: E.dma_start(
                out=_zr.view().rearrange("p (c r) -> p c r", c=C2),
                in_=zsp_d[:, :, _rc * 128:(_rc + 1) * 128].rearrange(
                    "c p r -> p c r")),
                reads=[f"zsp{hp}_{(rc * 128 // QG) * QG}" for hp in range(C2)],
                writes=[zr.k()], dma=True)
            for half in range(NHO):
                pp, kk = pf[2 + half % 2], f"pf{2 + half % 2}"
                fns = []
                for hp in range(C2):
                    fns.append(lambda E, _hp=hp, _rc=rc, _h=half, _pp=pp,
                               _zr=zr: E.matmul(
                        _pp[:, 0:HW],
                        _zr.ap(_hp * 128, (_hp + 1) * 128),
                        woT.ap(_hp * D + _h * HW, _hp * D + (_h + 1) * HW),
                        start=(_hp == 0), stop=(_hp == C2 - 1)))
                b.group("tensor", fns, reads=[zr.k(), woT.k()], writes=[kk])
                if half % 2 == 0:
                    b.op("vector", lambda E, _sl=sl, _pp=pp, _h=half:
                         E.tensor_copy(
                             sstage[_sl].ap(_h * HW, (_h + 1) * HW),
                             _pp[:, 0:HW]),
                         reads=[kk], writes=[sstage[sl].k(str(half))])
                else:
                    b.op("scalar", lambda E, _sl=sl, _pp=pp, _h=half: E.copy(
                        sstage[_sl].ap(_h * HW, (_h + 1) * HW), _pp[:, 0:HW]),
                        reads=[kk], writes=[sstage[sl].k(str(half))])
            b.op("sync", lambda E, _sl=sl, _rc=rc: E.dma_start(
                out=sdram[_rc * 128:(_rc + 1) * 128, :],
                in_=sstage[_sl].view()),
                reads=[sstage[sl].k(str(h)) for h in range(NHO)],
                writes=[f"sd{rc}"], dma=True)

        for nm in ("qTsl0", "qTsl1", "kTw0", "kTw1", "vw0", "vw1", "woT",
                   "zst0", "zst1", "zrc0", "zrc1"):
            ABF.free(nm)
        for nm in [f"patt{i}" for i in range(4)]:
            ABF.free(nm)
        AF32.free("sstage0")
        AF32.free("sstage1")

        we_r = [ABF.alloc(f"we{i}", KI * D) for i in range(2)]
        xnT_dh = ABF.alloc("xnT_dh", KI * SO)
        xnd_keys = []
        for ki in range(KI):
            kxk = xnT_dh.k(f"{ki}")
            xnd_keys.append(kxk)
            b.op("sync", lambda E, _ki=ki: E.dma_start(
                out=xnT_dh.ap(_ki * SO, (_ki + 1) * SO),
                in_=xnsp_d[:, _ki, 0:SO]),
                reads=[f"xnsp{ki}_{a}" for a in range(NAP)],
                writes=[kxk], dma=True)
        accr2 = [AF32.alloc(f"accr{i}", NRH * D) for i in range(2)]
        srow = [AF32.alloc(f"srow{i}", D) for i in range(2)]
        xrow = [AF32.alloc(f"xrow{i}", D) for i in range(2)]
        xnrow = AF32.alloc("xnrow", D)
        gsil = AF32.alloc("gsil", D)

        GL = gltmp
        for rc in range(RCO):
            fns = []
            for ki in range(KI):
                fns.append(lambda E, _ki=ki, _rc=rc: E.matmul(
                    pf[6][:, 0:NE],
                    xnT_dh.ap(_ki * SO + _rc * 128, _ki * SO + (_rc + 1) * 128),
                    gw_sb[:, _ki * NE:(_ki + 1) * NE],
                    start=(_ki == 0), stop=(_ki == KI - 1)))
            b.group("tensor", fns, reads=xnd_keys + ["gw_sb"], writes=["pf6"])
            b.op("vector", lambda E: E.tensor_tensor(
                GL[:, 0:NE], pf[6][:, 0:NE], gb_bc[:, :], ALU.add),
                reads=["pf6", "gb_bc"], writes=["glA"])
            b.op("vector", lambda E: E.reduce_max(
                GL[:, NE:NE + 1], GL[:, 0:NE], AX.X),
                reads=["glA"], writes=["glB"])
            b.op("vector", lambda E: E.tensor_single_scalar(
                GL[:, NE + 2:2 * NE + 2], GL[:, 0:NE], GL[:, NE:NE + 1],
                ALU.is_equal),
                reads=["glA", "glB"], writes=["glD"])
            b.op("vector", lambda E: E.scalar_tensor_tensor(
                GL[:, 2 * NE + 2:3 * NE + 2], GL[:, NE + 2:2 * NE + 2],
                -1e30, GL[:, 0:NE], ALU.mult, ALU.add),
                reads=["glD", "glA"], writes=["glE"])
            b.op("vector", lambda E: E.reduce_max(
                GL[:, NE + 1:NE + 2], GL[:, 2 * NE + 2:3 * NE + 2], AX.X),
                reads=["glE"], writes=["glC"])
            b.op("vector", lambda E: E.tensor_single_scalar(
                GL[:, 2 * NE + 2:3 * NE + 2], GL[:, 0:NE],
                GL[:, NE + 1:NE + 2], ALU.is_ge),
                reads=["glA", "glC"], writes=["glE"])
            b.op("vector", lambda E: E.tensor_single_scalar(
                GL[:, 0:NE], GL[:, 0:NE], GL[:, NE:NE + 1], ALU.subtract),
                reads=["glA", "glB"], writes=["glA"])
            b.op("scalar", lambda E: E.activation(
                GL[:, NE + 2:2 * NE + 2], GL[:, 0:NE], AF.Exp),
                reads=["glA"], writes=["glD"])
            b.op("vector", lambda E: E.tensor_tensor(
                GL[:, NE + 2:2 * NE + 2], GL[:, NE + 2:2 * NE + 2],
                GL[:, 2 * NE + 2:3 * NE + 2], ALU.mult),
                reads=["glD", "glE"], writes=["glD"])
            b.op("vector", lambda E: E.reduce_sum(
                GL[:, NE:NE + 1], GL[:, NE + 2:2 * NE + 2], AX.X),
                reads=["glD"], writes=["glB"])
            b.op("vector", lambda E: E.reciprocal(
                GL[:, NE + 1:NE + 2], GL[:, NE:NE + 1]),
                reads=["glB"], writes=["glC"])
            b.op("vector", lambda E, _rc=rc: E.tensor_single_scalar(
                g_all[:, _rc * NE:(_rc + 1) * NE], GL[:, NE + 2:2 * NE + 2],
                GL[:, NE + 1:NE + 2], ALU.mult),
                reads=["glD", "glC"], writes=[f"g{rc}"])
            b.op("vector", lambda E, _rc=rc: E.tensor_copy(
                gbf[:, :], g_all[:, _rc * NE:(_rc + 1) * NE]),
                reads=[f"g{rc}"], writes=["gbf"])
            b.op("tensor", lambda E: E.transpose(
                pbt[0:NE, 0:128], gbf[:, :], ident[:, :]),
                reads=["gbf", "ident"], writes=["pbt"])
            b.op("vector", lambda E, _rc=rc: E.tensor_copy(
                gt_sb[0:NE, _rc * 128:(_rc + 1) * 128], pbt[0:NE, 0:128]),
                reads=["pbt"], writes=[f"gt{rc}"])

        we_loaded = [None, None]
        for dh in range(NDH):
            rc_lo, rc_hi = dh * NRH, (dh + 1) * NRH
            accr = accr2[dh % 2]
            for rc in range(rc_lo, rc_hi):
                ar = rc - rc_lo
                for half in range(NHO):
                    b.op("tensor", lambda E, _rc=rc, _h=half: E.matmul(
                        pf[6][:, 0:HW],
                        gt_sb[0:NE, _rc * 128:(_rc + 1) * 128],
                        eb_sb[0:NE, _h * HW:(_h + 1) * HW],
                        start=True, stop=True),
                        reads=[f"gt{rc}", "eb_sb"], writes=["pf6"])
                    b.op("vector", lambda E, _a=ar, _h=half, _ac=accr:
                         E.tensor_copy(
                             _ac.ap(_a * D + _h * HW, _a * D + (_h + 1) * HW),
                             pf[6][:, 0:HW]),
                         reads=["pf6"], writes=[accr.k(f"{ar}_{half}")])
            eorder = list(range(NE)) if dh % 2 == 0 else \
                list(range(NE - 1, -1, -1))
            for e in eorder:
                esl = e % 2
                if we_loaded[esl] != e:
                    KQ = KI // 4
                    b.group("sync", [
                        (lambda E, _e=e, _s=esl, _k0=k0: E.dma_start(
                            out=we_r[_s].view().rearrange(
                                "p (k o) -> p k o", k=KI)[:, _k0:_k0 + KQ],
                            in_=webf_d[_e][_k0:_k0 + KQ].rearrange(
                                "k p o -> p k o")))
                        for k0 in range(0, KI, KQ)],
                        reads=[f"webf{e}"], writes=[we_r[esl].k()], dma=True)
                    we_loaded[esl] = e
                for rc in range(rc_lo, rc_hi):
                    ar = rc - rc_lo
                    pi0 = 2 + 2 * (rc % 2)
                    fns = []
                    for ki in range(KI):
                        for half in range(NHO):
                            fns.append(lambda E, _ki=ki, _rc=rc, _h=half,
                                       _pp=pf[pi0 + half], _s=esl: E.matmul(
                                _pp[:, 0:HW],
                                xnT_dh.ap(_ki * SO + _rc * 128,
                                          _ki * SO + (_rc + 1) * 128),
                                we_r[_s].ap(_ki * D + _h * HW,
                                            _ki * D + (_h + 1) * HW),
                                start=(_ki == 0), stop=(_ki == KI - 1)))
                    b.group("tensor", fns,
                            reads=xnd_keys + [we_r[esl].k()],
                            writes=[f"pf{pi0}", f"pf{pi0 + 1}"])
                    for half in range(NHO):
                        kk = f"pf{pi0 + half}"
                        b.op("vector", lambda E, _a=ar, _h=half, _rc=rc, _e=e,
                             _pp=pf[pi0 + half], _ac=accr:
                             E.scalar_tensor_tensor(
                                 _ac.ap(_a * D + _h * HW,
                                        _a * D + (_h + 1) * HW),
                                 _pp[:, 0:HW],
                                 g_all[:, _rc * NE + _e:_rc * NE + _e + 1],
                                 _ac.ap(_a * D + _h * HW,
                                        _a * D + (_h + 1) * HW),
                                 ALU.mult, ALU.add),
                             reads=[kk, f"g{rc}", accr.k(f"{ar}_{half}")],
                             writes=[accr.k(f"{ar}_{half}")])
            for rc in range(rc_lo, rc_hi):
                ar = rc - rc_lo
                sl = rc % 2
                b.op("sync", lambda E, _sl=sl, _rc=rc: E.dma_start(
                    out=srow[_sl].view(),
                    in_=sdram[_rc * 128:(_rc + 1) * 128, :]),
                    reads=[f"sd{rc}"], writes=[srow[sl].k()], dma=True)
                b.op("sync", lambda E, _sl=sl, _rc=rc: E.dma_start(
                    out=xrow[_sl].view(),
                    in_=x_d[_rc * 128:(_rc + 1) * 128, :]),
                    writes=[xrow[sl].k()], dma=True)
                b.op("vector", lambda E, _sl=sl, _rc=rc: E.tensor_scalar(
                    xnrow.view(), xrow[_sl].view(),
                    stats_rs[:, _rc:_rc + 1], stats_nb[:, _rc:_rc + 1],
                    ALU.mult, ALU.add),
                    reads=[xrow[sl].k(), f"rs{rc}", f"nb{rc}"],
                    writes=[xnrow.k()])
                b.op("scalar", lambda E, _a=ar, _ac=accr: E.activation(
                    gsil.view(), _ac.ap(_a * D, (_a + 1) * D), AF.Silu),
                    reads=[accr.k(f"{ar}_{h}") for h in range(NHO)],
                    writes=[gsil.k()])
                b.op("vector", lambda E, _sl=sl: E.tensor_tensor(
                    srow[_sl].view(), srow[_sl].view(), xnrow.view(),
                    ALU.add),
                    reads=[srow[sl].k(), xnrow.k()], writes=[srow[sl].k()])
                b.op("gpsimd", lambda E, _sl=sl: E.tensor_tensor(
                    gsil.view(), srow[_sl].view(), gsil.view(), ALU.mult),
                    reads=[srow[sl].k(), gsil.k()], writes=[gsil.k()])
                b.op("gpsimd", lambda E, _sl=sl: E.tensor_tensor(
                    srow[_sl].view(), gsil.view(), xrow[_sl].view(), ALU.add),
                    reads=[gsil.k(), xrow[sl].k()], writes=[srow[sl].k()])
                b.op("sync", lambda E, _o=sl, _rc=rc: E.dma_start(
                    out=out_d[_rc * 128:(_rc + 1) * 128, :],
                    in_=srow[_o].view()),
                    reads=[srow[sl].k()], writes=[f"out{rc}"], dma=True)

        b.schedule_and_emit()

        with nc.Block() as block:
            for eng in ENGINES:
                getattr(block, eng)(
                    lambda E, _eng=eng: [f(E) for f in b.cmds[_eng]])

    return nc



def _perm_blocks(cfg, parity):
    RC = cfg["RC"]
    return list(range(parity, RC, 2)) + list(range(1 - parity, RC, 2))


def _rope_tables(cfg, perm):
    S, DH = cfg["S"], 64
    inv = 1.0 / (10000.0 ** (np.arange(0, DH, 2, dtype=np.float64) / DH))
    t = np.arange(S, dtype=np.float64)
    fr = np.outer(t, inv)
    emb = np.concatenate([fr, fr], axis=1)
    cos = (np.cos(emb) / 8.0).astype(np.float32)
    sin = (np.sin(emb) / 8.0).astype(np.float32)
    rowperm = np.concatenate([np.arange(b * 128, (b + 1) * 128) for b in perm])
    cos = cos[rowperm]
    sin = sin[rowperm]
    cosT = np.ascontiguousarray(np.concatenate([cos.T, cos.T], axis=0))
    sinT = np.ascontiguousarray(np.concatenate([sin.T, sin.T], axis=0))
    return cosT, sinT


def _consts(cfg, parity):
    QG = cfg["QG"]
    nqb = QG // 128
    ident = np.eye(128, dtype=np.float32).astype(BF)
    bones = np.zeros((128, 128), np.float32)
    bones[0:64, 0:64] = 1.0
    bones[64:128, 64:128] = 1.0
    bones = bones.astype(BF)
    R = np.zeros((64, 64), np.float32)
    for i in range(0, 64, 2):
        R[i, i + 1] = -1.0
        R[i + 1, i] = 1.0
    RT = np.zeros((128, 128), np.float32)
    RT[0:64, 0:64] = R.T
    RT[64:128, 64:128] = R.T
    masks = np.zeros((2, nqb, 128, QG), np.float32)
    tri = np.tril(np.ones((128, 128), np.float32)).T
    for p in range(nqb):
        for j in range(nqb):
            blk = slice(j * 128, (j + 1) * 128)
            if j > p:
                masks[0][p][:, blk] = 1.0
            elif j == p:
                masks[0][p][:, blk] = tri
            if (p < j) or (parity == 1 and p == j):
                masks[1][p][:, blk] = 1.0
    return ident, bones, np.ascontiguousarray(RT).astype(BF), masks.astype(BF)


def _core_inputs(cfg, core, ins):
    S, D, KI, C2 = cfg["S"], cfg["D"], cfg["KI"], cfg["C2"]
    NE = cfg["NE"]
    b_idx, parity = core // 2, core % 2
    f32 = np.float32
    perm = _perm_blocks(cfg, parity)
    rowperm = np.concatenate([np.arange(b * 128, (b + 1) * 128) for b in perm])
    x = np.ascontiguousarray(
        np.asarray(ins["L0_attn_x"], dtype=f32)[b_idx][rowperm])

    def wT(W):
        t = np.ascontiguousarray(np.asarray(W, dtype=f32).T)
        return np.ascontiguousarray(t.reshape(KI, 128, D))

    wq = wT(ins["L0_attn_Wq"])
    wk = wT(ins["L0_attn_Wk"])
    wv = wT(ins["L0_attn_Wv"])
    wo = np.ascontiguousarray(
        np.asarray(ins["L0_attn_Wo"], dtype=f32).T.reshape(C2, 128, D))
    ew = np.asarray(ins["L0_ffn_expert_w"], dtype=f32)
    we = np.ascontiguousarray(ew.reshape(NE, KI, 128, D))
    eb = np.ascontiguousarray(np.asarray(ins["L0_ffn_expert_b"], dtype=f32))
    gw = np.ascontiguousarray(
        np.asarray(ins["L0_ffn_gate_w"], dtype=f32).T.reshape(KI, 128, NE))
    gb = np.ascontiguousarray(np.broadcast_to(
        np.asarray(ins["L0_ffn_gate_b"], dtype=f32)[None, :], (128, NE)))
    bq = np.asarray(ins["L0_attn_bq"], dtype=f32)
    bk = np.asarray(ins["L0_attn_bk"], dtype=f32)
    bqT = np.ascontiguousarray(bq.reshape(C2, 128).T)
    bkT = np.ascontiguousarray(bk.reshape(C2, 128).T)
    bv = np.ascontiguousarray(np.broadcast_to(
        np.asarray(ins["L0_attn_bv"], dtype=f32)[None, :], (128, D)))
    nw = np.asarray(ins["L0_attn_norm_w"], dtype=f32)
    nw128 = np.ascontiguousarray(np.tile(nw, 2)[:, None])
    pidx = np.arange(64) ^ 1
    nwp128 = np.ascontiguousarray(np.tile(nw[pidx], 2)[:, None])
    cosT, sinT = _rope_tables(cfg, perm)
    ident, bones, RT, masks = _consts(cfg, parity)
    return {
        "x": x, "wq": wq, "wk": wk, "wv": wv, "wo": wo, "we": we,
        "gw": gw, "gb": gb, "eb": eb, "bqT": bqT, "bkT": bkT, "bv_bc": bv,
        "cosw": cosT, "sinw": sinT, "nw128": nw128, "nwp128": nwp128,
        "identbf": ident, "bonesbf": bones, "rotT": RT, "masks_h": masks,
    }


def _assemble(cfg, outs):
    S, D = cfg["S"], cfg["D"]
    B = cfg["NCORES"] // 2
    full = np.empty((B, S, D), np.float32)
    for b_idx in range(B):
        for parity in range(2):
            o = outs[2 * b_idx + parity]
            for i, blk in enumerate(range(parity, cfg["RC"], 2)):
                full[b_idx, blk * 128:(blk + 1) * 128] = \
                    o[i * 128:(i + 1) * 128]
    return full


def kernel(**inputs):
    cfg = make_cfg(S=4096, D=1024, NH=16, NE=8)
    nc = build_program(cfg)
    in_maps = [_core_inputs(cfg, c, inputs) for c in range(8)]
    res = run_bass_kernel_spmd(nc, in_maps, core_ids=list(range(8)))
    outs = [np.asarray(res.results[c]["out"]) for c in range(8)]
    return _assemble(cfg, outs).astype(np.float32)
